# version 03
# baseline (speedup 1.0000x reference)
import numpy as np
from contextlib import ExitStack

import concourse.bass as bass
import concourse.mybir as mybir
from concourse.bass_utils import run_bass_kernel_spmd

F32 = mybir.dt.float32
BF16 = mybir.dt.bfloat16
AF = mybir.ActivationFunctionType
ALU = mybir.AluOpType

B, S, D = 4, 4096, 1024
NCORE = 8
T = 2048
NCH = D // 128
RT3 = None


class Prog:
    def __init__(self):
        self.nc = bass.Bass("TRN2", target_bir_lowering=False)
        self.es = ExitStack()
        self.es0 = self.es
        self.sems = {}
        self._uid = 0
        self.block = None
        self.closed = False
        self._scr = {}

    def scratch(self, tag, shape, dtype):
        key = (tag, tuple(shape), str(dtype))
        if key not in self._scr:
            self._scr[key] = self.es0.enter_context(
                self.nc.sbuf_tensor(self.uid(tag), shape, dtype))
        return self._scr[key]

    def start(self):
        self.block = self.es.enter_context(self.nc.Block())
        return self

    def uid(self, tag):
        self._uid += 1
        return "%s_%d" % (tag, self._uid)

    def sb(self, tag, shape, dtype):
        return self.es.enter_context(self.nc.sbuf_tensor(self.uid(tag), shape, dtype))

    def psum(self, tag, shape, dtype=F32):
        return self.es.enter_context(self.nc.psum_tensor(self.uid(tag), shape, dtype))

    def din(self, name, shape, dtype=F32):
        if not hasattr(self, "_dins"):
            self._dins = {}
        if name in self._dins:
            return self._dins[name]
        ap = self.nc.dram_tensor(name, list(shape), dtype, kind="ExternalInput").ap()
        self._dins[name] = ap
        return ap

    def dout(self, name, shape, dtype=F32):
        return self.nc.dram_tensor(name, list(shape), dtype, kind="ExternalOutput").ap()

    def dint(self, name, shape, dtype=F32, **kw):
        return self.nc.dram_tensor(name, list(shape), dtype, kind="Internal", **kw).ap()

    def sem(self, name):
        if name not in self.sems:
            h = self.es0.enter_context(self.nc.semaphore(name))
            self.sems[name] = [h, 0]
        return self.sems[name]

    def inc(self, instr, name, k=1):
        s = self.sem(name)
        instr.then_inc(s[0], k)
        s[1] += k
        return s[1]

    def count(self, name):
        return self.sem(name)[1]

    def wait(self, eng, name, val=None):
        s = self.sem(name)
        v = s[1] if val is None else val
        if v > 0:
            eng.wait_ge(s[0], v)

    def load(self, dst, src, q=None, sem="dma", **kw):
        q = q or self.nc.sync
        return self.inc(q.dma_start(out=dst, in_=src, **kw), sem, 16)

    def loadc(self, dst, src, sem="dma"):
        return self.inc(self.nc.gpsimd.dma_start(out=dst, in_=src), sem, 16)

    def store(self, dst, src, q=None, sem="sto", **kw):
        q = q or self.nc.sync
        return self.inc(q.dma_start(out=dst, in_=src, **kw), sem, 16)

    def finish(self):
        self.wait(self.nc.sync, "sto")

    def done(self):
        if not self.closed:
            self.finish()
            self.es0.close()
            self.closed = True
        return self

    def run(self, in_maps, **kw):
        self.done()
        return run_bass_kernel_spmd(self.nc, in_maps, core_ids=list(range(NCORE)), **kw)

    def prealloc_ln(self):
        tmx = getattr(self, "ln_tmx", 2624)
        for tg in ("ln_rmu", "ln_rss", "ln_rt"):
            self.scratch(tg, [1, tmx], F32)
        self.scratch("ln_sq", [128, 1024], F32)
        self.scratch("ln_u", [128, 512], F32)

    def seg(self):
        from contextlib import contextmanager

        @contextmanager
        def _cm():
            old = self.es
            self.es = ExitStack()
            try:
                yield
            finally:
                self.es.close()
                self.es = old
        return _cm()

    def barrier(self):
        nc = self.nc
        engs = [("bar_t", nc.tensor), ("bar_v", nc.vector), ("bar_s", nc.scalar),
                ("bar_g", nc.gpsimd), ("bar_y", nc.sync)]
        for nm, e in engs:
            self.inc(e.nop(), nm, 1)
        counts = [(name, s[1]) for name, s in self.sems.items()]
        for nm, e in engs:
            for name, cnt in counts:
                if cnt > 0:
                    e.wait_ge(self.sems[name][0], cnt)

    def ag_pair(self, tag, src_sb, shape, dtype, q=None):
        nc = self.nc
        cin = self.dint(self.uid(tag + "_i"), shape, dtype)
        cout = self.dint(self.uid(tag + "_o"), [2] + list(shape), dtype,
                         addr_space="Local")
        self.store(cin, src_sb, q=q, sem="ccs")
        self.wait(nc.gpsimd, "ccs")
        self.inc(nc.gpsimd.collective_compute(
            "AllGather", ALU.bypass,
            replica_groups=[[0, 1], [2, 3], [4, 5], [6, 7]],
            ins=[cin.opt()], outs=[cout.opt()]), "cc")
        return cout

    def ag_dram(self, tag, src_ap, shape, dtype):
        nc = self.nc
        cout = self.dint(self.uid(tag + "_o"), [2] + list(shape), dtype,
                         addr_space="Local")
        self.inc(nc.gpsimd.collective_compute(
            "AllGather", ALU.bypass,
            replica_groups=[[0, 1], [2, 3], [4, 5], [6, 7]],
            ins=[src_ap.opt()], outs=[cout.opt()]), "cc")
        return cout


class PsumPool:
    def __init__(self, p, banks=8):
        self.p = p
        self.t = p.psum("pp", [128, 512 * banks], F32)
        self.banks = banks
        self.pend = [None] * banks
        self.rr = 0

    def view(self, b, rows=128, cols=512):
        return self.t[0:rows, 512 * b: 512 * b + cols]

    def acquire(self, eng, rows=128, cols=512):
        b = self.rr
        self.rr = (self.rr + 1) % self.banks
        if self.pend[b] is not None:
            name, cnt = self.pend[b]
            self.p.wait(eng, name, cnt)
            self.pend[b] = None
        return b, self.view(b, rows, cols)

    def release(self, b, sem):
        self.pend[b] = (sem, self.p.count(sem))

    def acquire_seq(self, eng, nb, rows=128):
        if self.rr + nb > self.banks:
            self.rr = 0
        b0 = self.rr
        self.rr = (self.rr + nb) % self.banks
        ids = list(range(b0, b0 + nb))
        for b in ids:
            if self.pend[b] is not None:
                name, cnt = self.pend[b]
                self.p.wait(eng, name, cnt)
                self.pend[b] = None
        return ids, self.t[0:rows, 512 * b0: 512 * (b0 + nb)]


def tile_w(W, cb=128):
    W = np.asarray(W)
    K, M = W.shape
    nk, nm = K // 128, (M + cb - 1) // cb
    assert nk * 128 == K and nm * cb == M, (W.shape, cb)
    return np.ascontiguousarray(
        W.reshape(nk, 128, nm, cb).transpose(2, 1, 0, 3).reshape(nm, 128, nk * cb))


def col128(v):
    v = np.asarray(v).reshape(-1)
    M = v.shape[0]
    return np.ascontiguousarray(v.reshape(M // 128, 128).T)


def host_consts():
    c = np.zeros((128, 262), np.float32)
    c[:, 0] = 1.0
    c[:, 34] = 1.0
    c[:, 35:163] = np.eye(128, dtype=np.float32)
    c[0:64, 163] = 1.0
    c[64:128, 195] = 1.0
    c[0:32, 196] = 1.0
    c[64:96, 196] = 1.0
    c[32:64, 228] = 1.0
    c[96:128, 228] = 1.0
    a_rows = np.zeros(128, np.float32)
    a_rows[0:32] = 1.0
    a_rows[64:96] = 1.0
    c[0, 229:229 + 33] = 0.0
    return c


CONSTS_W = 262


def host_selpb():
    m = np.zeros((33, 128), np.float32)
    m[0, 0:64] = 1.0
    m[32, 64:128] = 1.0
    return m


class Consts:
    pass


def load_consts(p):
    p.prealloc_ln()
    cin = p.din("consts", [128, CONSTS_W])
    csb = p.sb("csb", [128, CONSTS_W], F32)
    p.load(csb[:], cin[:, :])
    ones = p.sb("ones", [128, 128], F32)
    p.inc(p.nc.vector.memset(ones[:], 1.0), "dve")
    ones_bfc = p.sb("ones_bfc", [128, 1], BF16)
    p.inc(p.nc.vector.memset(ones_bfc[:], 1.0), "dve")
    ones_row = ones[0:1, :]
    sel_a = csb[:, 0:1]
    sel_b = csb[:, 2:35]
    eye = csb[:, 35:163]
    co = Consts()
    co.csb, co.sel_a, co.sel_b, co.eye, co.ones = csb, sel_a, sel_b, eye, ones
    co.ones_row = ones_row
    co.sel_h = csb[:, 163:196]
    co.sel_p = csb[:, 196:229]
    co.ones_bfc = ones_bfc
    return csb, sel_a, sel_b, eye, ones_row, co


def emit_ln_fm(p, pm, x_fn, T_, sel_a, sel_b, ones_row, out_fn=None, res_fn=None,
               nch=NCH, gamma=None, beta=None, eps=1e-5, mode="ln", ln_d=D,
               rstd_out=None, sel_x=None):
    nc = p.nc
    nslc = (T_ + 511) // 512
    TMX = getattr(p, "ln_tmx", 2624)
    assert T_ <= TMX
    r_mu = p.scratch("ln_rmu", [1, TMX], F32)[:, 0:T_]
    r_ss = p.scratch("ln_rss", [1, TMX], F32)[:, 0:T_]
    r_rstd = r_ss
    r_nmur = r_mu
    sqt = p.scratch("ln_sq", [128, 1024], F32)
    sq = [sqt[:, 0:512], sqt[:, 512:1024]]
    u = p.scratch("ln_u", [128, 512], F32)
    p.wait(nc.scalar, "pe")
    p.wait(nc.scalar, "dve")
    p.wait(nc.tensor, "dve")

    pe_done = {}
    for s in range(nslc):
        sl = slice(512 * s, min(512 * (s + 1), T_))
        w = sl.stop - sl.start
        bank_m, pv_m = pm.acquire(nc.tensor, rows=1, cols=w)
        bank_s, pv_s = pm.acquire(nc.tensor, rows=1, cols=w)
        for c in range(nch):
            key = (s, c)
            prev = pe_done.get((s, c - 2)) or (pe_done.get((s - 1, nch - 2 + c)) if s else None)
            if prev:
                p.wait(nc.scalar, "pe", prev)
            p.inc(nc.scalar.activation(out=sq[c % 2][:, 0:w], in_=x_fn(c)[:, sl],
                                       func=AF.Square), "act")
            p.wait(nc.tensor, "act")
            if mode == "ln":
                nc.tensor.matmul(pv_m, sel_x if sel_x is not None else sel_a,
                                 x_fn(c)[:, sl],
                                 start=(c == 0), stop=(c == nch - 1))
            i2 = nc.tensor.matmul(pv_s, sel_a, sq[c % 2][:, 0:w],
                                  start=(c == 0), stop=(c == nch - 1))
            pe_done[key] = p.inc(i2, "pe")
        p.wait(nc.vector, "pe")
        if mode == "ln":
            p.inc(nc.vector.tensor_copy(r_mu[:, sl], pv_m), "dve")
        p.inc(nc.vector.tensor_copy(r_ss[:, sl], pv_s), "dve")
        pm.release(bank_m, "dve")
        pm.release(bank_s, "dve")

    r_t = p.scratch("ln_rt", [1, TMX], F32)[:, 0:T_]
    if mode == "ln":
        p.inc(nc.vector.tensor_scalar(r_mu[:], r_mu[:], 1.0 / ln_d, None, ALU.mult), "dve")
        p.wait(nc.scalar, "dve")
        p.inc(nc.scalar.activation(out=r_t[:], in_=r_mu[:], func=AF.Square), "act")
        p.wait(nc.vector, "act")
        p.inc(nc.vector.scalar_tensor_tensor(
            r_ss[:], r_ss[:], 1.0 / ln_d, r_t[:], ALU.mult, ALU.subtract), "dve")
        p.inc(nc.vector.tensor_scalar(r_ss[:], r_ss[:], eps, None, ALU.add), "dve")
        p.wait(nc.scalar, "dve")
        p.inc(nc.scalar.activation(out=r_ss[:], in_=r_ss[:], func=AF.Ln), "act")
        p.inc(nc.scalar.activation(out=r_ss[:], in_=r_ss[:], func=AF.Exp,
                                   scale=-0.5), "act")
        p.wait(nc.vector, "act")
        p.inc(nc.vector.scalar_tensor_tensor(
            r_mu[:], r_mu[:], -1.0, r_ss[:], ALU.mult, ALU.mult), "dve")
    else:
        p.inc(nc.vector.tensor_scalar(r_ss[:], r_ss[:], 1e-24, None, ALU.max), "dve")
        p.wait(nc.scalar, "dve")
        p.inc(nc.scalar.activation(out=r_ss[:], in_=r_ss[:], func=AF.Ln,
                                   scale=1.0 / ln_d), "act")
        p.inc(nc.scalar.activation(out=r_ss[:], in_=r_ss[:], func=AF.Exp,
                                   scale=-0.5), "act")
        p.inc(nc.vector.memset(r_mu[:], 0.0), "dve")
    if rstd_out is not None:
        p.wait(nc.vector, "dve")
        p.wait(nc.vector, "act")
        p.inc(nc.vector.tensor_copy(rstd_out, r_ss[:]), "dve")

    for s in range(nslc):
        sl = slice(512 * s, min(512 * (s + 1), T_))
        w = sl.stop - sl.start
        b1, v1 = pm.acquire(nc.tensor, cols=w)
        p.wait(nc.tensor, "dve")
        p.wait(nc.tensor, "act")
        nc.tensor.matmul(v1, ones_row[:, :], r_rstd[:, sl], start=True, stop=True)
        b2, v2 = pm.acquire(nc.tensor, cols=w)
        p.inc(nc.tensor.matmul(v2, ones_row[:, :], r_nmur[:, sl], start=True, stop=True),
              "pe")
        p.wait(nc.vector, "pe")
        for c in range(nch):
            p.inc(nc.vector.tensor_tensor(u[:, 0:w], x_fn(c)[:, sl], v1, ALU.mult), "dve")
            if gamma is not None:
                p.inc(nc.vector.tensor_tensor(u[:, 0:w], u[:, 0:w], v2, ALU.add), "dve")
                if beta is None:
                    i = nc.vector.tensor_scalar(out_fn(c)[:, sl], u[:, 0:w],
                                                gamma(c), None, ALU.mult)
                else:
                    i = nc.vector.tensor_scalar(out_fn(c)[:, sl], u[:, 0:w],
                                                gamma(c), beta(c), ALU.mult, ALU.add)
                p.inc(i, "dve")
            elif out_fn is not None:
                p.inc(nc.vector.tensor_tensor(out_fn(c)[:, sl], u[:, 0:w], v2, ALU.add),
                      "dve")
            if res_fn is not None:
                src = out_fn(c)[:, sl] if out_fn is not None else None
                if src is None:
                    p.inc(nc.vector.tensor_tensor(u[:, 0:w], u[:, 0:w], v2, ALU.add), "dve")
                    src = u[:, 0:w]
                p.inc(nc.vector.tensor_tensor(res_fn(c)[:, sl], res_fn(c)[:, sl], src,
                                              ALU.add), "dve")
        pm.release(b1, "dve")
        pm.release(b2, "dve")


def emit_proj_stream(p, pm, wdram, nmc, nkc, cw, rhs_fn, Tw, evac_cb, wbufs, tag,
                     state=None, nsl=1, cast=True):
    nc = p.nc
    st = state if state is not None else {}
    for mc in range(nmc):
        buf = wbufs[mc % 2]
        key = id(buf)
        if key in st:
            p.wait(nc.gpsimd, "pe", st[key])
        if cast:
            p.loadc(buf[:], wdram[mc, :, :], sem="wdma")
        else:
            p.load(buf[:], wdram[mc, :, :], q=nc.gpsimd, sem="wdma")
        p.wait(nc.tensor, "wdma")
        for s in range(nsl):
            bank, pv = pm.acquire(nc.tensor, cols=Tw)
            for kc in range(nkc):
                i = nc.tensor.matmul(pv, buf[:, cw * kc:cw * (kc + 1)], rhs_fn(kc, s),
                                     start=(kc == 0), stop=(kc == nkc - 1))
            st[key] = p.inc(i, "pe")
            sem = evac_cb(mc, s, pv)
            pm.release(bank, sem)
    return st


def core_slice(c):
    return (c // 2, slice((c % 2) * T, (c % 2) * T + T))


def shard_stream(x_t):
    out = []
    for c in range(NCORE):
        b, sl = core_slice(c)
        out.append(np.ascontiguousarray(x_t[b, :, sl]))
    return out


def gather_stream(parts, key="ys"):
    x = np.empty((B, D, S), np.float32)
    for c in range(NCORE):
        b, sl = core_slice(c)
        x[b, :, sl] = parts[c][key]
    return x


def seg_L0_ffn(p, pm, co, din, xin, xout):
    nc = p.nc
    w1t = din("w1t", [32, 128, 8 * 128])
    b1r = din("b1r", [128, 32])
    w2t = din("w2t", [8, 128, 32 * 128])
    b2r = din("b2r", [128, 8])
    with p.seg():
        X = [p.sb("X", [128, T], F32) for _ in range(NCH)]
        H = [p.sb("H", [128, T], BF16) for _ in range(NCH)]
        G1 = [p.sb("G1", [128, 512], BF16) for _ in range(32)]
        Yt = [p.sb("Yt", [128, 512], F32) for _ in range(NCH)]
        W1b = [p.sb("W1b", [128, 8 * 128], BF16) for _ in range(2)]
        W2b = [p.sb("W2b", [128, 32 * 128], BF16) for _ in range(2)]
        b1sb = p.sb("b1sb", [128, 32], F32)
        b2sb = p.sb("b2sb", [128, 8], F32)

        p.load(b1sb[:], b1r[:, :])
        p.load(b2sb[:], b2r[:, :])
        for c in range(NCH):
            p.load(X[c][:], xin[128 * c:128 * (c + 1), :])
        for e in (nc.scalar, nc.vector, nc.tensor):
            p.wait(e, "dma")

        emit_ln_fm(p, pm, lambda c: X[c][:], T, co.sel_a, co.sel_b, co.ones_row,
                   out_fn=lambda c: H[c][:])

        st = {}
        for ts in range(T // 512):
            sl = slice(512 * ts, 512 * (ts + 1))

            def evac1(mc, s, pv, _sl=sl):
                p.wait(nc.scalar, "pe")
                p.inc(nc.scalar.activation(out=G1[mc][:], in_=pv, func=AF.Relu,
                                           bias=b1sb[:, mc:mc + 1], scale=1.0), "act")
                return "act"

            emit_proj_stream(p, pm, w1t, 32, 8, 128,
                             lambda kc, s, _sl=sl: H[kc][:, _sl], 512, evac1, W1b,
                             "w1", st)
            p.wait(nc.tensor, "act")

            def evac2(oc, s, pv):
                p.wait(nc.scalar, "pe")
                p.inc(nc.scalar.activation(out=Yt[oc][:], in_=pv, func=AF.Identity,
                                           bias=b2sb[:, oc:oc + 1], scale=1.0), "act")
                return "act"

            emit_proj_stream(p, pm, w2t, 8, 32, 128,
                             lambda kc, s: G1[kc][:, 0:512], 512, evac2, W2b, "w2", st)
            p.wait(nc.vector, "act")
            p.wait(nc.scalar, "act")
            emit_ln_fm(p, pm, lambda c: Yt[c][:], 512, co.sel_a, co.sel_b, co.ones_row,
                       res_fn=lambda c, _sl=sl: X[c][:, _sl])

        p.wait(nc.sync, "dve")
        for c in range(NCH):
            p.store(xout[128 * c:128 * (c + 1), :], X[c][:])
        p.barrier()


def build_L0_ffn():
    p = Prog()
    xs = p.din("xs", [D, T])
    ys = p.dout("ys", [D, T])
    p.start()
    csb, sel_a, sel_b, eye, ones_row, co = load_consts(p)
    pm = PsumPool(p, banks=8)
    seg_L0_ffn(p, pm, co, lambda n, s, d=F32: p.din(n, s, d), xs, ys)
    return p


def prep_L0_ffn(inp):
    cst = host_consts()
    w1t = tile_w(inp["L0_ffn_W1"])
    b1r = col128(inp["L0_ffn_b1"])
    w2t = tile_w(inp["L0_ffn_W2"])
    b2r = col128(inp["L0_ffn_b2"])
    return {"w1t": w1t, "b1r": b1r, "w2t": w2t, "b2r": b2r, "consts": cst}


def run_L0_ffn(p, x_t, inp):
    shared = prep_L0_ffn(inp)
    shards = shard_stream(x_t)
    in_maps = [dict(shared, xs=shards[c]) for c in range(NCORE)]
    res = p.run(in_maps)
    return gather_stream(res.results)


def seg_U(p, pm, co, xin, uout, dtype=BF16):
    nc = p.nc
    with p.seg():
        X = [p.sb("X", [128, T], F32) for _ in range(NCH)]
        U = [p.sb("U", [128, T], dtype) for _ in range(NCH)]
        for c in range(NCH):
            p.load(X[c][:], xin[128 * c:128 * (c + 1), :])
        for e in (nc.scalar, nc.vector, nc.tensor):
            p.wait(e, "dma")
        emit_ln_fm(p, pm, lambda c: X[c][:], T, co.sel_a, co.sel_b, co.ones_row,
                   out_fn=lambda c: U[c][:])
        p.wait(nc.sync, "dve")
        for c in range(NCH):
            p.store(uout[128 * c:128 * (c + 1), :], U[c][:])
        p.barrier()


def build_U():
    p = Prog()
    xs = p.din("xs", [D, T])
    us = p.dout("us", [D, T], BF16)
    p.start()
    csb, sel_a, sel_b, eye, ones_row, co = load_consts(p)
    co.sel_a, co.sel_b, co.ones_row = sel_a, sel_b, ones_row
    pm = PsumPool(p, banks=8)
    seg_U(p, pm, co, xs, us, BF16)
    return p


def run_U(p, x_t):
    shards = shard_stream(x_t)
    cst = host_consts()
    res = p.run([{"xs": shards[c], "consts": cst} for c in range(NCORE)])
    import ml_dtypes
    u_t = np.empty((B, D, S), ml_dtypes.bfloat16)
    for c in range(NCORE):
        b, sl = core_slice(c)
        u_t[b, :, sl] = res.results[c]["us"]
    return u_t


def seg_L1a_scan(p, pm, din, ufo, hsl, state):
    nc = p.nc
    bwt = din("bwt", [2, 128, 8 * 128])
    lamr = din("lamr", [128, 2])
    seg_cm = p.seg()
    seg_cm.__enter__()
    U = [p.sb("U", [128, T], BF16) for _ in range(NCH)]
    Bu = [p.sb("Bu", [128, T], F32) for _ in range(2)]
    lamf = p.sb("lamf", [128, T], F32)
    Wbw = [p.sb("Wbw", [128, 8 * 128], BF16) for _ in range(2)]
    lam = p.sb("lam", [128, 2], F32)
    stt_sb = p.sb("stt", [128, 2], F32)

    for c in range(NCH):
        p.load(U[c][:], ufo[128 * c:128 * (c + 1), :])
    p.load(lam[:], lamr[:, :])
    p.loadc(Wbw[0][:], bwt[0, :, :])
    p.loadc(Wbw[1][:], bwt[1, :, :])
    p.wait(nc.scalar, "dma")
    p.inc(nc.scalar.activation(out=lam[:], in_=lam[:], func=AF.Sigmoid), "act")
    p.wait(nc.tensor, "dma")

    for mc in range(2):
        for s in range(4):
            sl = slice(512 * s, 512 * (s + 1))
            bank, pv = pm.acquire(nc.tensor)
            for kc in range(NCH):
                i = nc.tensor.matmul(pv, Wbw[mc][:, 128 * kc:128 * (kc + 1)],
                                     U[kc][:, sl], start=(kc == 0), stop=(kc == NCH - 1))
            p.inc(i, "pe")
            p.wait(nc.vector, "pe")
            p.inc(nc.vector.tensor_copy(Bu[mc][:, sl], pv), "dve")
            pm.release(bank, "dve")
        p.wait(nc.vector, "dve")
        p.inc(nc.vector.memset(lamf[:], 1.0), "dve")
        p.wait(nc.vector, "dve")
        p.wait(nc.vector, "act")
        p.inc(nc.vector.tensor_scalar(lamf[:], lamf[:], lam[:, mc:mc + 1], None, ALU.mult),
              "dve")
        p.inc(nc.vector.tensor_tensor_scan(Bu[mc][:], lamf[:], Bu[mc][:], 0.0,
                                           ALU.mult, ALU.add), "dve")
        p.wait(nc.vector, "dve")
        p.inc(nc.vector.tensor_copy(stt_sb[:, mc:mc + 1], Bu[mc][:, T - 1:T]), "dve")
        p.wait(nc.sync, "dve")
        p.store(hsl[mc, :, :], Bu[mc][:])
    p.wait(nc.sync, "dve")
    p.store(state[:, :], stt_sb[:])
    p.barrier()
    seg_cm.__exit__(None, None, None)


def build_L1a_scan():
    p = Prog()
    ufo = p.din("ufo", [D, T], BF16)
    hsl = p.dout("hsl", [2, 128, T])
    state = p.dout("state", [128, 2])
    p.start()
    pm = PsumPool(p, banks=8)
    seg_L1a_scan(p, pm, lambda n, s, d=F32: p.din(n, s, d), ufo, hsl, state)
    return p


def seg_L1a_out(p, pm, co, din, xin, xout, ufo, hsl, st_in, hfc=None):
    nc = p.nc
    sel_a, sel_b, ones_row = co.sel_a, co.sel_b, co.ones_row
    lamr = din("lamr", [128, 2])
    cwt = din("cwt", [8, 128, 2 * 128])
    dvr = din("dvr", [128, 8])
    xs = xin
    seg_cm = p.seg()
    seg_cm.__enter__()
    X = [p.sb("X", [128, T], F32) for _ in range(NCH)]
    U = [p.sb("U", [128, T], BF16) for _ in range(NCH)]
    HS = [p.sb("HS", [128, T], F32) for _ in range(2)]
    lamf = p.sb("lamf", [128, T], F32)
    zt = p.sb("zt", [128, T], F32)
    corr = p.sb("corr", [128, T], F32)
    Yt = [p.sb("Yt", [128, 512], F32) for _ in range(NCH)]
    Cw = [p.sb("Cw", [128, 2 * 128], F32) for _ in range(NCH)]
    lam = p.sb("lam", [128, 2], F32)
    stc = p.sb("stc", [128, 2], F32)
    dvc = p.sb("dvc", [128, 8], F32)

    for c in range(NCH):
        p.load(X[c][:], xs[128 * c:128 * (c + 1), :])
        p.load(U[c][:], ufo[128 * c:128 * (c + 1), :])
        p.load(Cw[c][:], cwt[c, :, :])
    for mc in range(2):
        p.load(HS[mc][:], hsl[mc, :, :])
    p.load(lam[:], lamr[:, :])
    p.load(stc[:], st_in[:, :])
    p.load(dvc[:], dvr[:, :])
    p.wait(nc.scalar, "dma")
    p.inc(nc.scalar.activation(out=lam[:], in_=lam[:], func=AF.Sigmoid), "act")
    p.wait(nc.vector, "dma")
    if hfc is not None:
        p.inc(nc.vector.tensor_scalar(stc[:], stc[:], hfc, None, ALU.mult), "dve")
    p.inc(nc.vector.memset(zt[:], 0.0), "dve")

    for mc in range(2):
        p.wait(nc.vector, "dve")
        p.inc(nc.vector.memset(lamf[:], 1.0), "dve")
        p.wait(nc.vector, "dve")
        p.wait(nc.vector, "act")
        p.inc(nc.vector.tensor_scalar(lamf[:], lamf[:], lam[:, mc:mc + 1], None, ALU.mult),
              "dve")
        p.inc(nc.vector.tensor_tensor_scan(corr[:], lamf[:], zt[:], stc[:, mc:mc + 1],
                                           ALU.mult, ALU.add), "dve")
        p.wait(nc.vector, "dve")
        p.inc(nc.vector.tensor_tensor(HS[mc][:], HS[mc][:], corr[:], ALU.add), "dve")

    p.wait(nc.tensor, "dve")
    for s in range(4):
        sl = slice(512 * s, 512 * (s + 1))
        for oc in range(NCH):
            bank, pv = pm.acquire(nc.tensor)
            for kc in range(2):
                i = nc.tensor.matmul(pv, Cw[oc][:, 128 * kc:128 * (kc + 1)],
                                     HS[kc][:, sl], start=(kc == 0), stop=(kc == 1))
            p.inc(i, "pe")
            p.wait(nc.vector, "pe")
            p.inc(nc.vector.scalar_tensor_tensor(
                Yt[oc][:], U[oc][:, sl], dvc[:, oc:oc + 1], pv, ALU.mult, ALU.add), "dve")
            pm.release(bank, "dve")
        emit_ln_fm(p, pm, lambda c: Yt[c][:], 512, sel_a, sel_b, ones_row,
                   res_fn=lambda c, _sl=sl: X[c][:, _sl])

    p.wait(nc.sync, "dve")
    for c in range(NCH):
        p.store(xout[128 * c:128 * (c + 1), :], X[c][:])
    p.barrier()
    seg_cm.__exit__(None, None, None)


def build_L1a_out():
    p = Prog()
    p.ln_tmx = 512
    xs = p.din("xs", [D, T])
    ufo = p.din("ufo", [D, T], BF16)
    hsl = p.din("hsl", [2, 128, T])
    st_in = p.din("st_in", [128, 2])
    ys = p.dout("ys", [D, T])
    p.start()
    csb, sel_a, sel_b, eye, ones_row, co = load_consts(p)
    pm = PsumPool(p, banks=8)
    seg_L1a_out(p, pm, co, lambda n, s, d=F32: p.din(n, s, d), xs, ys, ufo, hsl,
                st_in)
    return p


def run_L1_attn(pU, pA, pB, x_t, inp):
    shards = shard_stream(x_t)
    cst = host_consts()
    resU = pU.run([{"xs": shards[c], "consts": cst} for c in range(NCORE)])
    ufo = [np.ascontiguousarray(resU.results[c]["us"]) for c in range(NCORE)]
    bwt = tile_w(np.ascontiguousarray(np.asarray(inp["L1_attn_B_w"]).T))
    cwt = tile_w(np.ascontiguousarray(np.asarray(inp["L1_attn_C_w"]).T))
    lamr = col128(inp["L1_attn_log_lambda"])
    dvr = col128(inp["L1_attn_D"])
    resA = pA.run([{"ufo": ufo[c], "bwt": bwt, "lamr": lamr} for c in range(NCORE)])
    zero_state = np.zeros((128, 2), np.float32)
    in_maps = []
    for c in range(NCORE):
        st_in = zero_state if c % 2 == 0 else np.ascontiguousarray(
            resA.results[c - 1]["state"])
        in_maps.append({"xs": shards[c], "ufo": ufo[c],
                        "hsl": np.ascontiguousarray(resA.results[c]["hsl"]),
                        "st_in": st_in, "lamr": lamr, "cwt": cwt, "dvr": dvr,
                        "consts": cst})
    resB = pB.run(in_maps)
    return gather_stream(resB.results)


def seg_L0_attn(p, pm, co, din, ps_o, ps_b, xin, xout, ufo, uf2):
    nc = p.nc
    sel_a, sel_b, ones_row, csb = co.sel_a, co.sel_b, co.ones_row, co.csb
    wbd = din("wbd", [128, 3 * 128])
    wot = din("wot", [8, 128, 8 * 128])
    bor = din("bor", [128, 8])
    seg_cm = p.seg()
    seg_cm.__enter__()
    X = [p.sb("X", [128, T], F32) for _ in range(NCH)]
    O = [p.sb("Ofm", [128, T], BF16) for _ in range(NCH)]
    _uf = p.sb("UF", [128, S], BF16)
    UF = [_uf, _uf]
    _uo = p.sb("UO", [128, T], BF16)
    UO = [_uo, _uo]
    Qc = p.sb("Qc", [128, T], BF16)
    Kc = p.sb("Kc", [128, S], BF16)
    Vc = p.sb("Vc", [128, 32 * 130], BF16)
    Ab = [p.sb("Ab", [128, 2048], BF16) for _ in range(2)]
    _sqt = p.scratch("ln_sq", [128, 1024], F32)
    sq2 = [_sqt[:, 0:512], _sqt[:, 512:1024]]
    qm = p.sb("qm", [33, 16], F32)
    cc = p.sb("cc", [33, 1], F32)
    bias_sb = p.sb("bias_sb", [128, 2], F32)
    dn = p.sb("dn", [128, 512], F32)
    wsb = p.sb("wsb", [128, 3 * 128], BF16)
    Wo_b = [p.sb("Wo_b", [128, 8 * 128], BF16) for _ in range(2)]
    Yt = [p.sb("Yt", [128, 512], F32) for _ in range(NCH)]
    bo_sb = p.sb("bo_sb", [128, 8], F32)

    for c in range(NCH):
        p.load(X[c][:], xin[128 * c:128 * (c + 1), :])
    p.loadc(wsb[:], wbd[:, :])
    p.load(bo_sb[:], bor[:, :])
    p.inc(nc.vector.memset(Vc[:], 1.0), "dve")
    for e in (nc.scalar, nc.vector, nc.tensor):
        p.wait(e, "dma")

    ld_st = {}
    po_free = 0
    for ch in range(NCH):
        UFc, UOc = UF[ch % 2], UO[ch % 2]
        if ch >= 1:
            p.wait(nc.sync, "pe", ld_st[(ch - 1) % 2])
        p.load(UFc[:, 0:T], uf2[0, 128 * ch:128 * (ch + 1), :])
        p.load(UFc[:, T:S], uf2[1, 128 * ch:128 * (ch + 1), :])
        p.load(UOc[:], ufo[128 * ch:128 * (ch + 1), :])
        p.wait(nc.tensor, "dma")
        p.wait(nc.tensor, "dve")
        for s in range(4):
            bank, pv = pm.acquire(nc.tensor)
            p.inc(nc.tensor.matmul(pv, wsb[:, 0:128], UOc[:, 512 * s:512 * (s + 1)],
                                   start=True, stop=True), "pe")
            p.wait(nc.vector, "pe")
            p.inc(nc.vector.tensor_copy(Qc[:, 512 * s:512 * (s + 1)], pv), "dve")
            pm.release(bank, "dve")
        for s in range(8):
            bank, pv = pm.acquire(nc.tensor)
            p.inc(nc.tensor.matmul(pv, wsb[:, 128:256], UFc[:, 512 * s:512 * (s + 1)],
                                   start=True, stop=True), "pe")
            p.wait(nc.vector, "pe")
            p.inc(nc.vector.tensor_copy(Kc[:, 512 * s:512 * (s + 1)], pv), "dve")
            pm.release(bank, "dve")
        for i in range(32):
            bank, pv = pm.acquire(nc.tensor, cols=128)
            p.inc(nc.tensor.matmul(pv, UFc[:, 128 * i:128 * (i + 1)], wsb[:, 256:384],
                                   start=True, stop=True), "pe")
            p.wait(nc.vector, "pe")
            dst = Vc[:, 130 * i:130 * (i + 1)].rearrange("p (h e) -> p h e", e=65)[:, :, 0:64]
            srcv = pv.rearrange("p (h e) -> p h e", e=64)
            p.inc(nc.vector.tensor_copy(dst, srcv), "dve")
            pm.release(bank, "dve")
        ld_st[ch % 2] = p.count("pe")

        p.wait(nc.scalar, "dve")
        sqg = {}
        for s in range(12):
            src_t = (Qc[:, 512 * s:512 * (s + 1)] if s < 4
                     else Kc[:, 512 * (s - 4):512 * (s - 3)])
            if s >= 2:
                p.wait(nc.scalar, "pe", sqg[s - 2])
            p.inc(nc.scalar.activation(out=sq2[s % 2][:], in_=src_t, func=AF.Square),
                  "act")
            bank, pv = pm.acquire(nc.tensor, rows=33)
            p.wait(nc.tensor, "act")
            sqg[s] = p.inc(nc.tensor.matmul(pv, co.sel_h, sq2[s % 2][:],
                                            start=True, stop=True), "pe")
            p.wait(nc.vector, "pe")
            p.inc(nc.vector.tensor_reduce(qm[:, s:s + 1], pv, mybir.AxisListType.X,
                                          ALU.max), "dve")
            pm.release(bank, "dve")
        p.inc(nc.vector.tensor_reduce(qm[:, 12:13], qm[:, 0:4], mybir.AxisListType.X,
                                      ALU.max), "dve")
        p.inc(nc.vector.tensor_reduce(qm[:, 13:14], qm[:, 4:12], mybir.AxisListType.X,
                                      ALU.max), "dve")
        p.wait(nc.vector, "dve")
        p.wait(nc.vector, "dve")
        p.inc(nc.vector.tensor_tensor(cc[:], qm[:, 12:13], qm[:, 13:14], ALU.mult), "dve")
        p.wait(nc.scalar, "dve")
        p.inc(nc.scalar.activation(out=cc[:], in_=cc[:], func=AF.Sqrt), "act")
        p.wait(nc.vector, "act")
        p.inc(nc.vector.tensor_scalar(cc[:], cc[:], -1.0 / 32.0, None, ALU.mult), "dve")
        p.wait(nc.tensor, "dve")
        nc.tensor.matmul(ps_b[:, 0:1], co.ones[0:1, :], cc[0:1, 0:1],
                         start=True, stop=True)
        p.inc(nc.tensor.matmul(ps_b[:, 1:2], co.ones[32:33, :], cc[32:33, 0:1],
                               start=True, stop=True), "pe")
        p.wait(nc.scalar, "pe")
        p.inc(nc.scalar.copy(bias_sb[:], ps_b[:, 0:2]), "act")

        for hb in (0, 64):
            hi = hb // 64
            for qs in range(4):
            
                qsl = slice(512 * qs, 512 * (qs + 1))
                first = True
                for g in range(8):
                    banks, pv4 = pm.acquire_seq(nc.tensor, 4)
                    p.wait(nc.tensor, "act")
                    for j in range(4):
                        kt = 4 * g + j
                        i = nc.tensor.matmul(
                            pv4[:, 512 * j:512 * (j + 1)],
                            Kc[hb:hb + 64, 128 * kt:128 * (kt + 1)],
                            Qc[hb:hb + 64, qsl], start=True, stop=True)
                    p.inc(i, "pe")
                    p.wait(nc.scalar, "pe")
                    p.inc(nc.scalar.activation(out=Ab[g % 2][:], in_=pv4, func=AF.Exp,
                                               bias=bias_sb[:, hi:hi + 1],
                                               scale=1.0 / 32.0), "act")
                    for b_ in banks:
                        pm.release(b_, "act")
                    if first:
                        p.wait(nc.tensor, "dve", po_free)
                    p.wait(nc.tensor, "act")
                    for j in range(4):
                        kt = 4 * g + j
                        i = nc.tensor.matmul(
                            ps_o[0:65, :],
                            Vc[:, 130 * kt + 65 * hi:130 * kt + 65 * (hi + 1)],
                            Ab[g % 2][:, 512 * j:512 * (j + 1)],
                            start=(first and j == 0), stop=(g == 7 and j == 3))
                    first = False
                p.inc(i, "pe")
                p.wait(nc.scalar, "pe")
                p.inc(nc.scalar.activation(out=dn[64:65, 0:512], in_=ps_o[64:65, :],
                                           func=AF.Ln), "act")
                p.inc(nc.scalar.activation(out=dn[64:65, 0:512], in_=dn[64:65, 0:512],
                                           func=AF.Exp, scale=-1.0), "act")
                p.wait(nc.tensor, "dve")
                p.wait(nc.tensor, "act")
                i = nc.tensor.matmul(ps_b[0:64, :], co.ones[64:65, 0:64],
                                     dn[64:65, 0:512], start=True, stop=True)
                p.inc(i, "pe")
                p.wait(nc.scalar, "pe")
                p.inc(nc.scalar.copy(dn[0:64, 0:512], ps_b[0:64, :]), "act")
                p.wait(nc.vector, "act")
                p.inc(nc.vector.tensor_tensor(O[ch][hb:hb + 64, qsl], ps_o[0:64, :],
                                              dn[0:64, 0:512], ALU.mult), "dve")
                po_free = p.count("dve")

    st = {}
    for ts in range(4):
        sl = slice(512 * ts, 512 * (ts + 1))

        def evac(oc, s, pv, _sl=sl):
            p.wait(nc.scalar, "pe")
            p.inc(nc.scalar.activation(out=Yt[oc][:], in_=pv, func=AF.Identity,
                                       bias=bo_sb[:, oc:oc + 1], scale=1.0), "act")
            return "act"

        p.wait(nc.tensor, "dve")
        emit_proj_stream(p, pm, wot, 8, 8, 128,
                         lambda kc, s, _sl=sl: O[kc][:, _sl], 512, evac, Wo_b, "wo", st)
        p.wait(nc.vector, "act")
        emit_ln_fm(p, pm, lambda c: Yt[c][:], 512, sel_a, sel_b, ones_row,
                   res_fn=lambda c, _sl=sl: X[c][:, _sl])

    p.wait(nc.sync, "dve")
    for c in range(NCH):
        p.store(xout[128 * c:128 * (c + 1), :], X[c][:])
    p.barrier()
    seg_cm.__exit__(None, None, None)


def build_L0_attn():
    p = Prog()
    xs = p.din("xs", [D, T])
    ufo = p.din("ufo", [D, T], BF16)
    uf2 = p.din("uf2", [2, D, T], BF16)
    ys = p.dout("ys", [D, T])
    p.ln_tmx = 512
    p.start()
    csb, sel_a, sel_b, eye, ones_row, co = load_consts(p)
    pm = PsumPool(p, banks=6)
    ps_o = p.psum("ps_o", [128, 512], F32)
    ps_b = p.psum("ps_b", [128, 512], F32)
    seg_L0_attn(p, pm, co, lambda n, s, d=F32: p.din(n, s, d), ps_o, ps_b,
                xs, ys, ufo, uf2)
    return p


def run_L0_attn(pU, pA, x_t, inp):
    shards = shard_stream(x_t)
    cst = host_consts()
    resU = pU.run([{"xs": shards[c], "consts": cst} for c in range(NCORE)])
    import ml_dtypes
    u_t = np.empty((B, D, S), ml_dtypes.bfloat16)
    for c in range(NCORE):
        b, sl = core_slice(c)
        u_t[b, :, sl] = resU.results[c]["us"]
    wbd = np.zeros((128, 3 * 128), np.float32)
    for j, nm in enumerate(["L0_attn_Wq", "L0_attn_Wk", "L0_attn_Wv"]):
        W = np.asarray(inp[nm], np.float32)
        wbd[0:64, 128 * j:128 * j + 64] = W
        wbd[64:128, 128 * j + 64:128 * j + 128] = W
    wot = tile_w(inp["L0_attn_Wo"])
    bor = col128(inp["L0_attn_bo"])
    in_maps = []
    for c in range(NCORE):
        b, sl = core_slice(c)
        in_maps.append({"xs": shards[c],
                        "ufo": np.ascontiguousarray(u_t[b, :, sl]),
                        "uf2": np.ascontiguousarray(
                            u_t[b].reshape(D, 2, T).transpose(1, 0, 2)),
                        "wbd": wbd, "wot": wot, "bor": bor, "consts": cst})
    res = pA.run(in_maps)
    return gather_stream(res.results)


def seg_L2a_qkv(p, pm, din, ufo, qs_o, ks_o, vs_o):
    nc = p.nc
    wqt = din("wqt", [8, 128, 8 * 128])
    wkt = din("wkt", [8, 128, 8 * 128])
    wvn = din("wvn", [D, D])
    bqr = din("bqr", [128, 8])
    bkr = din("bkr", [128, 8])
    bvrow = din("bvrow", [1, D])
    seg_cm = p.seg()
    seg_cm.__enter__()
    U = [p.sb("U", [128, T], BF16) for _ in range(NCH)]
    Wv = [p.sb("Wv", [128, D], BF16) for _ in range(NCH)]
    QK = [p.sb("QK", [128, 512], BF16) for _ in range(2)]
    VS = [p.sb("VS", [128, 1040], BF16) for _ in range(2)]
    Wb = [p.sb("Wb", [128, 8 * 128], BF16) for _ in range(2)]
    bq_sb = p.sb("bq_sb", [128, 8], F32)
    bk_sb = p.sb("bk_sb", [128, 8], F32)
    bv_sb = p.sb("bv_sb", [1, D], BF16)
    ones_bf = p.sb("ones_bf", [1, 128], BF16)

    for c in range(NCH):
        p.load(U[c][:], ufo[128 * c:128 * (c + 1), :])
        p.loadc(Wv[c][:], wvn[128 * c:128 * (c + 1), :])
    p.load(bq_sb[:], bqr[:, :])
    p.load(bk_sb[:], bkr[:, :])
    p.loadc(bv_sb[:], bvrow[:, :])
    p.inc(nc.vector.memset(ones_bf[:], 1.0), "dve")
    p.inc(nc.vector.memset(VS[0][:], 1.0), "dve")
    p.inc(nc.vector.memset(VS[1][:], 1.0), "dve")
    for e in (nc.scalar, nc.vector, nc.tensor):
        p.wait(e, "dma")
        p.wait(e, "dve")

    st = {}
    vsq = {}
    ev_state = {"n": 0, "sto": {}}
    for side, wt, bb, dst in (("q", wqt, bq_sb, qs_o), ("k", wkt, bk_sb, ks_o)):
        def evac(mc, s, pv, _b=bb, _d=dst):
            n = ev_state["n"]
            ev_state["n"] += 1
            buf = QK[n % 2]
            p.wait(nc.vector, "pe")
            if n >= 2:
                p.wait(nc.vector, "sto", ev_state["sto"][n % 2])
            p.inc(nc.vector.tensor_scalar(buf[:], pv, _b[:, mc:mc + 1], None, ALU.add),
                  "dve")
            p.wait(nc.sync, "dve")
            ev_state["sto"][n % 2] = p.store(
                _d[128 * mc:128 * (mc + 1), 512 * s:512 * (s + 1)], buf[:])
            return "dve"
        emit_proj_stream(p, pm, wt, 8, 8, 128,
                         lambda kc, s: U[kc][:, 512 * s:512 * (s + 1)], 512,
                         evac, Wb, "w" + side, st, nsl=4)
    for i in range(16):
        vbuf = VS[i % 2]
        if i >= 2:
            p.wait(nc.vector, "sto", vsq[i % 2])
        for half in range(2):
            bank, pv = pm.acquire(nc.tensor)
            for kc in range(NCH):
                nc.tensor.matmul(pv, U[kc][:, 128 * i:128 * (i + 1)],
                                 Wv[kc][:, 512 * half:512 * (half + 1)],
                                 start=(kc == 0), stop=False)
            i2 = nc.tensor.matmul(pv, ones_bf[:, 0:128],
                                  bv_sb[:, 512 * half:512 * (half + 1)],
                                  start=False, stop=True)
            p.inc(i2, "pe")
            p.wait(nc.vector, "pe")
            dstv = vbuf[:, 520 * half:520 * (half + 1)].rearrange(
                "p (h e) -> p h e", e=65)[:, :, 0:64]
            srcv = pv.rearrange("p (h e) -> p h e", e=64)
            p.inc(nc.vector.tensor_copy(dstv, srcv), "dve")
            pm.release(bank, "dve")
        p.wait(nc.sync, "dve")
        vsq[i % 2] = p.store(vs_o[i, :, :], vbuf[:])
    p.barrier()
    seg_cm.__exit__(None, None, None)


def build_L2a_qkv():
    p = Prog()
    ufo = p.din("ufo", [D, T], BF16)
    qs_o = p.dout("qs_o", [D, T], BF16)
    ks_o = p.dout("ks_o", [D, T], BF16)
    vs_o = p.dout("vs_o", [16, 128, 1040], BF16)
    p.start()
    pm = PsumPool(p, banks=8)
    seg_L2a_qkv(p, pm, lambda n, s, d=F32: p.din(n, s, d), ufo, qs_o, ks_o, vs_o)
    return p


def seg_L2a_attn(p, pm, co, din, ps_o, ps_b, xin, xout, qs, kprev, kown,
                 vprev, vown):
    nc = p.nc
    sel_a, sel_b, ones_row = co.sel_a, co.sel_b, co.ones_row
    padm = din("padm", [128, 32])
    m4 = din("m4", [128, 4 * 512])
    wot = din("wot", [8, 128, 8 * 128])
    bor = din("bor", [128, 8])
    xs = xin
    seg_cm = p.seg()
    seg_cm.__enter__()
    X = [p.sb("X", [128, T], F32) for _ in range(NCH)]
    O = [p.sb("Ofm", [128, T], BF16) for _ in range(NCH)]
    Qc = [p.sb("Qc", [128, T], BF16) for _ in range(2)]
    Kc = [p.sb("Kc", [128, S], BF16) for _ in range(2)]
    Vc = [p.sb("Vc", [128, 32 * 130], BF16) for _ in range(2)]
    Ab = [p.sb("Ab", [128, 2048], BF16) for _ in range(2)]
    _sqt = p.scratch("ln_sq", [128, 1024], F32)
    sq2 = [_sqt[:, 0:512], _sqt[:, 512:1024]]
    qm = p.sb("qm", [33, 16], F32)
    cc = p.sb("cc", [33, 1], F32)
    bias_sb = p.sb("bias_sb", [128, 2], F32)
    biasH = p.sb("biasH", [128, 64], F32)
    padm_sb = p.sb("padm_sb", [128, 32], F32)
    mask_sb = p.sb("mask_sb", [128, 2048], BF16)
    dn = p.sb("dn", [128, 512], F32)
    Wo_b = [p.sb("Wo_b", [128, 8 * 128], BF16) for _ in range(2)]
    Yt = [p.sb("Yt", [128, 512], F32) for _ in range(NCH)]
    bo_sb = p.sb("bo_sb", [128, 8], F32)

    for c in range(NCH):
        p.load(X[c][:], xs[128 * c:128 * (c + 1), :])
    p.load(padm_sb[:], padm[:, :])
    p.loadc(mask_sb[:], m4[:, :])
    p.load(bo_sb[:], bor[:, :])
    for e in (nc.scalar, nc.vector, nc.tensor):
        p.wait(e, "dma")

    ld_st = {}
    po_free = 0
    for ch in range(NCH):
        Qs, Ks, Vs = Qc[ch % 2], Kc[ch % 2], Vc[ch % 2]
        if ch >= 2:
            p.wait(nc.sync, "pe", ld_st.get(ch % 2, 0))
        p.load(Qs[:], qs[128 * ch:128 * (ch + 1), :])
        p.load(Ks[:, 0:T], kprev[128 * ch:128 * (ch + 1), :])
        p.load(Ks[:, T:S], kown[128 * ch:128 * (ch + 1), :])
        p.load(Vs[:, 0:16 * 130],
               vprev[:, :, 130 * ch:130 * (ch + 1)].rearrange("i p v -> p i v"))
        p.load(Vs[:, 16 * 130:32 * 130],
               vown[:, :, 130 * ch:130 * (ch + 1)].rearrange("i p v -> p i v"))
        p.wait(nc.scalar, "dma")
        p.wait(nc.tensor, "dma")
        sqg = {}
        for s in range(12):
            src_t = (Qs[:, 512 * s:512 * (s + 1)] if s < 4
                     else Ks[:, 512 * (s - 4):512 * (s - 3)])
            if s >= 2:
                p.wait(nc.scalar, "pe", sqg[s - 2])
            p.inc(nc.scalar.activation(out=sq2[s % 2][:], in_=src_t, func=AF.Square),
                  "act")
            bank, pv = pm.acquire(nc.tensor, rows=33)
            p.wait(nc.tensor, "act")
            sqg[s] = p.inc(nc.tensor.matmul(pv, co.sel_h, sq2[s % 2][:],
                                            start=True, stop=True), "pe")
            p.wait(nc.vector, "pe")
            p.inc(nc.vector.tensor_reduce(qm[:, s:s + 1], pv, mybir.AxisListType.X,
                                          ALU.max), "dve")
            pm.release(bank, "dve")
        p.inc(nc.vector.tensor_reduce(qm[:, 12:13], qm[:, 0:4], mybir.AxisListType.X,
                                      ALU.max), "dve")
        p.inc(nc.vector.tensor_reduce(qm[:, 13:14], qm[:, 4:12], mybir.AxisListType.X,
                                      ALU.max), "dve")
        p.wait(nc.vector, "dve")
        p.inc(nc.vector.tensor_tensor(cc[:], qm[:, 12:13], qm[:, 13:14], ALU.mult), "dve")
        p.wait(nc.scalar, "dve")
        p.inc(nc.scalar.activation(out=cc[:], in_=cc[:], func=AF.Sqrt), "act")
        p.wait(nc.vector, "act")
        p.inc(nc.vector.tensor_scalar(cc[:], cc[:], -1.0 / 8.0, None, ALU.mult), "dve")
        p.wait(nc.tensor, "dve")
        p.wait(nc.tensor, "act")
        nc.tensor.matmul(ps_b[:, 0:1], co.ones[0:1, :], cc[0:1, 0:1],
                         start=True, stop=True)
        p.inc(nc.tensor.matmul(ps_b[:, 1:2], co.ones[32:33, :], cc[32:33, 0:1],
                               start=True, stop=True), "pe")
        p.wait(nc.scalar, "pe")
        p.inc(nc.scalar.copy(bias_sb[:], ps_b[:, 0:2]), "act")
        p.wait(nc.vector, "act")
        for hi in range(2):
            p.inc(nc.vector.tensor_scalar(biasH[:, 32 * hi:32 * (hi + 1)], padm_sb[:],
                                          bias_sb[:, hi:hi + 1], None, ALU.add), "dve")

        for hb in (0, 64):
            hi = hb // 64
            for qsl_i in range(4):
                qsl = slice(512 * qsl_i, 512 * (qsl_i + 1))
                ng = 5 + qsl_i
                first = True
                for g in range(ng):
                    banks, pv4 = pm.acquire_seq(nc.tensor, 4)
                    p.wait(nc.tensor, "act")
                    p.wait(nc.tensor, "dve")
                    for j in range(4):
                        kt = 4 * g + j
                        i = nc.tensor.matmul(
                            pv4[:, 512 * j:512 * (j + 1)],
                            Ks[hb:hb + 64, 128 * kt:128 * (kt + 1)],
                            Qs[hb:hb + 64, qsl], start=True, stop=True)
                    p.inc(i, "pe")
                    p.wait(nc.scalar, "pe")
                    for j in range(4):
                        kt = 4 * g + j
                        p.inc(nc.scalar.activation(
                            out=Ab[g % 2][:, 512 * j:512 * (j + 1)],
                            in_=pv4[:, 512 * j:512 * (j + 1)], func=AF.Exp,
                            bias=biasH[:, 32 * hi + kt:32 * hi + kt + 1],
                            scale=1.0 / 8.0), "act")
                    for b_ in banks:
                        pm.release(b_, "act")
                    if g == ng - 1:
                        p.wait(nc.vector, "act")
                        p.inc(nc.vector.tensor_tensor(Ab[g % 2][:], Ab[g % 2][:],
                                                      mask_sb[:], ALU.mult), "dve")
                    if first:
                        p.wait(nc.tensor, "dve", po_free)
                    p.wait(nc.tensor, "act")
                    if g == ng - 1:
                        p.wait(nc.tensor, "dve")
                    for j in range(4):
                        kt = 4 * g + j
                        i = nc.tensor.matmul(
                            ps_o[0:65, :],
                            Vs[:, 130 * kt + 65 * hi:130 * kt + 65 * (hi + 1)],
                            Ab[g % 2][:, 512 * j:512 * (j + 1)],
                            start=(first and j == 0), stop=(g == ng - 1 and j == 3))
                    first = False
                p.inc(i, "pe")
                p.wait(nc.scalar, "pe")
                p.inc(nc.scalar.activation(out=dn[64:65, 0:512], in_=ps_o[64:65, :],
                                           func=AF.Ln), "act")
                p.inc(nc.scalar.activation(out=dn[64:65, 0:512], in_=dn[64:65, 0:512],
                                           func=AF.Exp, scale=-1.0), "act")
                p.wait(nc.tensor, "dve")
                p.wait(nc.tensor, "act")
                i = nc.tensor.matmul(ps_b[0:64, :], co.ones[64:65, 0:64],
                                     dn[64:65, 0:512], start=True, stop=True)
                p.inc(i, "pe")
                p.wait(nc.scalar, "pe")
                p.inc(nc.scalar.copy(dn[0:64, 0:512], ps_b[0:64, :]), "act")
                p.wait(nc.vector, "act")
                p.inc(nc.vector.tensor_tensor(O[ch][hb:hb + 64, qsl], ps_o[0:64, :],
                                              dn[0:64, 0:512], ALU.mult), "dve")
                po_free = p.count("dve")
        ld_st[ch % 2] = p.count("pe")

    st = {}
    for ts in range(4):
        sl = slice(512 * ts, 512 * (ts + 1))

        def evac(oc, s, pv):
            p.wait(nc.scalar, "pe")
            p.inc(nc.scalar.activation(out=Yt[oc][:], in_=pv, func=AF.Identity,
                                       bias=bo_sb[:, oc:oc + 1], scale=1.0), "act")
            return "act"

        p.wait(nc.tensor, "dve")
        emit_proj_stream(p, pm, wot, 8, 8, 128,
                         lambda kc, s, _sl=sl: O[kc][:, _sl], 512, evac, Wo_b, "wo", st)
        p.wait(nc.vector, "act")
        emit_ln_fm(p, pm, lambda c: Yt[c][:], 512, sel_a, sel_b, ones_row,
                   res_fn=lambda c, _sl=sl: X[c][:, _sl])

    p.wait(nc.sync, "dve")
    for c in range(NCH):
        p.store(xout[128 * c:128 * (c + 1), :], X[c][:])
    p.barrier()
    seg_cm.__exit__(None, None, None)


def build_L2a_attn():
    p = Prog()
    xs = p.din("xs", [D, T])
    qs = p.din("qs", [D, T], BF16)
    kprev = p.din("kprev", [D, T], BF16)
    kown = p.din("kown", [D, T], BF16)
    vprev = p.din("vprev", [16, 128, 1040], BF16)
    vown = p.din("vown", [16, 128, 1040], BF16)
    ys = p.dout("ys", [D, T])
    p.ln_tmx = 512
    p.start()
    csb, sel_a, sel_b, eye, ones_row, co = load_consts(p)
    pm = PsumPool(p, banks=6)
    ps_o = p.psum("ps_o", [128, 512], F32)
    ps_b = p.psum("ps_b", [128, 512], F32)
    seg_L2a_attn(p, pm, co, lambda n, s, d=F32: p.din(n, s, d), ps_o, ps_b,
                 xs, ys, qs, kprev, kown, vprev, vown)
    return p


def host_diag_masks():
    mu = np.triu(np.ones((128, 128), np.float32))
    m = np.zeros((128, 4 * 512), np.float32)
    for mm in range(4):
        blk = np.zeros((128, 512), np.float32)
        for j in range(4):
            if j > mm:
                blk[:, 128 * j:128 * (j + 1)] = 1.0
            elif j == mm:
                blk[:, 128 * j:128 * (j + 1)] = mu
        m[:, 512 * mm:512 * (mm + 1)] = blk
    return m


def run_L2_attn(pU, pQ, pA, x_t, inp):
    shards = shard_stream(x_t)
    cst = host_consts()
    resU = pU.run([{"xs": shards[c], "consts": cst} for c in range(NCORE)])
    ufo = [np.ascontiguousarray(resU.results[c]["us"]) for c in range(NCORE)]
    Wq = np.asarray(inp["L2_attn_Wq"], np.float32).transpose(1, 0, 2).reshape(D, D)
    Wk = np.asarray(inp["L2_attn_Wk"], np.float32).transpose(1, 0, 2).reshape(D, D)
    Wv = np.asarray(inp["L2_attn_Wv"], np.float32).transpose(1, 0, 2).reshape(D, D)
    qkv_in = {"wqt": tile_w(Wq), "wkt": tile_w(Wk), "wvn": np.ascontiguousarray(Wv),
              "bqr": col128(np.asarray(inp["L2_attn_bq"]).reshape(-1)),
              "bkr": col128(np.asarray(inp["L2_attn_bk"]).reshape(-1)),
              "bvrow": np.ascontiguousarray(
                  np.asarray(inp["L2_attn_bv"], np.float32).reshape(1, D))}
    resQ = pQ.run([dict(qkv_in, ufo=ufo[c]) for c in range(NCORE)])
    import ml_dtypes
    bf = ml_dtypes.bfloat16
    wot = tile_w(inp["L2_attn_Wo"])
    bor = col128(inp["L2_attn_bo"])
    m4 = host_diag_masks()
    in_maps = []
    for c in range(NCORE):
        b, half = c // 2, c % 2
        kprev = np.ascontiguousarray(resQ.results[2 * b]["ks_o"])
        vprev = np.ascontiguousarray(resQ.results[2 * b]["vs_o"])
        if half == 0:
            padv = np.concatenate([np.full((128, 16), -30000.0, np.float32),
                                   np.zeros((128, 16), np.float32)], axis=1)
        else:
            padv = np.zeros((128, 32), np.float32)
        in_maps.append({"xs": shards[c], "qs": resQ.results[c]["qs_o"],
                        "kprev": kprev, "kown": np.ascontiguousarray(
                            resQ.results[c]["ks_o"]),
                        "vprev": vprev, "vown": np.ascontiguousarray(
                            resQ.results[c]["vs_o"]),
                        "padm": padv, "m4": m4,
                        "wot": wot, "bor": bor, "consts": cst})
    res = pA.run(in_maps)
    return gather_stream(res.results)


def seg_L3a_local(p, pm, din, ps_oA, ps_oB, ps_s, ufo, oaug, qp_o, st_o):
    nc = p.nc
    wqkvt = din("wqkvt", [24, 128, 8 * 128])
    bqkr = din("bqkr", [128, 16])
    bvrow = din("bvrow", [1, D])
    projp = din("projp", [128, 8 * 64])
    mup = din("mup", [128, 128])
    wvn = din("wvn", [D, D])
    seg_cm = p.seg()
    seg_cm.__enter__()
    U = [p.sb("U", [128, T], BF16) for _ in range(NCH)]
    Wb = [p.sb("Wb", [128, 8 * 128], BF16) for _ in range(2)]
    Wv = [p.sb("Wv", [128, D], BF16) for _ in range(NCH)]
    Qch = p.sb("Qch", [128, T], BF16)
    Kch = p.sb("Kch", [128, T], BF16)
    Qp = p.sb("Qp", [128, T], BF16)
    Kpf = p.sb("Kpf", [128, T], BF16)
    Ktm = p.sb("Ktm", [128, T], BF16)
    Vc = p.sb("Vc", [128, 16 * 130], BF16)
    Asb = p.sb("Asb", [128, 128], BF16)
    Ssb = p.sb("Ssb", [128, 130], BF16)
    OA = [p.sb("OA", [65, 512], BF16) for _ in range(2)]
    projsb = p.sb("projsb", [128, 8 * 64], BF16)
    mupsb = p.sb("mupsb", [128, 128], BF16)
    bqk_sb = p.sb("bqk_sb", [128, 16], F32)
    bv_sb = p.sb("bv_sb", [1, D], BF16)
    ones_bf = p.sb("ones_bf", [1, 128], BF16)
    stt_sb = p.sb("stt_sb", [128, 8 * 130], BF16)

    for c in range(NCH):
        p.load(U[c][:], ufo[128 * c:128 * (c + 1), :])
    for c in range(NCH):
        p.loadc(Wv[c][:], wvn[128 * c:128 * (c + 1), :])
    p.loadc(projsb[:], projp[:, :])
    p.loadc(mupsb[:], mup[:, :])
    p.load(bqk_sb[:], bqkr[:, :])
    p.loadc(bv_sb[:], bvrow[:, :])
    p.inc(nc.vector.memset(ones_bf[:], 1.0), "dve")
    p.inc(nc.vector.memset(Vc[:], 1.0), "dve")
    for e in (nc.scalar, nc.vector, nc.tensor):
        p.wait(e, "dma")
        p.wait(e, "dve")

    st = {}
    oa_guard = {}
    oan = [0]
    for ch in range(NCH):
        for side, blk, dst in (("q", ch, Qch), ("k", 8 + ch, Kch)):
            buf = Wb[blk % 2]
            if id(buf) in st:
                p.wait(nc.gpsimd, "pe", st[id(buf)])
            p.loadc(buf[:], wqkvt[blk, :, :], sem="wdma")
            p.wait(nc.tensor, "wdma")
            bci = (0 if side == "q" else 8) + ch
            for s in range(4):
                bank, pv = pm.acquire(nc.tensor)
                for kc in range(NCH):
                    i = nc.tensor.matmul(pv, buf[:, 128 * kc:128 * (kc + 1)],
                                         U[kc][:, 512 * s:512 * (s + 1)],
                                         start=(kc == 0), stop=(kc == NCH - 1))
                st[id(buf)] = p.inc(i, "pe")
                p.wait(nc.scalar, "pe")
                p.inc(nc.scalar.activation(out=dst[:, 512 * s:512 * (s + 1)], in_=pv,
                                           func=AF.Identity,
                                           bias=bqk_sb[:, bci:bci + 1], scale=1.0),
                      "act")
                pm.release(bank, "act")
        p.wait(nc.tensor, "dve")
        for i_ in range(16):
            bank, pv = pm.acquire(nc.tensor, cols=128)
            for kc in range(NCH):
                nc.tensor.matmul(pv, U[kc][:, 128 * i_:128 * (i_ + 1)],
                                 Wv[kc][:, 128 * ch:128 * (ch + 1)],
                                 start=(kc == 0), stop=False)
            i2 = nc.tensor.matmul(pv, ones_bf[:, 0:128],
                                  bv_sb[:, 128 * ch:128 * (ch + 1)],
                                  start=False, stop=True)
            p.inc(i2, "pe")
            p.wait(nc.vector, "pe")
            dstv = Vc[:, 130 * i_:130 * (i_ + 1)].rearrange(
                "p (h e) -> p h e", e=65)[:, :, 0:64]
            p.inc(nc.vector.tensor_copy(dstv, pv.rearrange("p (h e) -> p h e", e=64)),
                  "dve")
            pm.release(bank, "dve")
        p.wait(nc.tensor, "act")
        for hi, hb in ((0, 0), (1, 64)):
            pj = projsb[hb:hb + 64, 64 * ch:64 * (ch + 1)]
            for s in range(4):
                for srcq, dstq in ((Qch, Qp), (Kch, Kpf)):
                    bank, pv = pm.acquire(nc.tensor)
                    p.inc(nc.tensor.matmul(pv[hb:hb + 64, :], pj,
                                           srcq[hb:hb + 64, 512 * s:512 * (s + 1)],
                                           start=True, stop=True), "pe")
                    p.wait(nc.scalar, "pe")
                    p.inc(nc.scalar.activation(
                        out=dstq[hb:hb + 64, 512 * s:512 * (s + 1)],
                        in_=pv[hb:hb + 64, :], func=AF.Relu, scale=0.125), "act")
                    pm.release(bank, "act")
            for i_ in range(16):
                bank, pv = pm.acquire(nc.tensor, cols=64)
                p.inc(nc.tensor.matmul(pv[0:128, 0:64],
                                       Kch[hb:hb + 64, 128 * i_:128 * (i_ + 1)], pj,
                                       start=True, stop=True), "pe")
                p.wait(nc.scalar, "pe")
                p.inc(nc.scalar.activation(
                    out=Ktm[:, 128 * i_ + 64 * hi:128 * i_ + 64 * (hi + 1)],
                    in_=pv[0:128, 0:64], func=AF.Relu, scale=0.125), "act")
                pm.release(bank, "act")
        p.wait(nc.sync, "act")
        p.store(qp_o[128 * ch:128 * (ch + 1), :], Qp[:])
        p.wait(nc.tensor, "act")
        p.wait(nc.tensor, "dve")
        for i_ in range(16):
            if i_ > 0:
                p.wait(nc.scalar, "pe")
                p.inc(nc.scalar.copy(Ssb[:], ps_s[:, 0:130]), "act")
                p.wait(nc.tensor, "act")
            for hi, hb in ((0, 0), (1, 64)):
                pso = ps_oA if hi == 0 else ps_oB
                bank, pv = pm.acquire(nc.tensor, cols=128)
                p.inc(nc.tensor.matmul(pv, Kpf[hb:hb + 64, 128 * i_:128 * (i_ + 1)],
                                       Qp[hb:hb + 64, 128 * i_:128 * (i_ + 1)],
                                       start=True, stop=True), "pe")
                p.wait(nc.vector, "pe")
                p.inc(nc.vector.tensor_tensor(Asb[:], pv, mupsb[:], ALU.mult), "dve")
                pm.release(bank, "dve")
                p.wait(nc.tensor, "dve")
                nc.tensor.matmul(pso[0:65, 128 * (i_ % 4):128 * (i_ % 4 + 1)],
                                 Vc[:, 130 * i_ + 65 * hi:130 * i_ + 65 * (hi + 1)],
                                 Asb[:], start=(i_ % 4 == 0), stop=False)
                if i_ > 0:
                    nc.tensor.matmul(pso[0:65, 128 * (i_ % 4):128 * (i_ % 4 + 1)],
                                     Ssb[64 * hi:64 * (hi + 1), 65 * hi:65 * (hi + 1)],
                                     Qp[hb:hb + 64, 128 * i_:128 * (i_ + 1)],
                                     start=False, stop=(i_ % 4 == 3))
                i3 = nc.tensor.matmul(
                    ps_s[64 * hi:64 * (hi + 1), 65 * hi:65 * (hi + 1)],
                    Ktm[:, 128 * i_ + 64 * hi:128 * i_ + 64 * (hi + 1)],
                    Vc[:, 130 * i_ + 65 * hi:130 * i_ + 65 * (hi + 1)],
                    start=(i_ == 0), stop=(i_ == 15 and hi == 1))
                p.inc(i3, "pe")
            if i_ % 4 == 3:
                g = i_ // 4
                for hi in range(2):
                    h = 2 * ch + hi
                    pso = ps_oA if hi == 0 else ps_oB
                    buf = OA[oan[0] % 2]
                    if oan[0] >= 2:
                        p.wait(nc.scalar, "sto", oa_guard[oan[0] % 2])
                    p.wait(nc.scalar, "pe")
                    p.inc(nc.scalar.copy(buf[:], pso[0:65, :]), "act")
                    p.wait(nc.sync, "act")
                    oa_guard[oan[0] % 2] = p.store(oaug[h, :, 512 * g:512 * (g + 1)],
                                                   buf[:])
                    oan[0] += 1
        p.wait(nc.vector, "pe")
        p.inc(nc.vector.tensor_copy(stt_sb[:, 130 * ch:130 * (ch + 1)], ps_s[:, 0:130]),
              "dve")
    p.wait(nc.sync, "dve")
    p.store(st_o[:, :], stt_sb[:])
    p.barrier()
    seg_cm.__exit__(None, None, None)


def build_L3a_local():
    p = Prog()
    ufo = p.din("ufo", [D, T], BF16)
    oaug = p.dout("oaug", [16, 65, T], BF16)
    qp_o = p.dout("qp_o", [D, T], BF16)
    st_o = p.dout("st_o", [128, 8 * 130], BF16)
    p.start()
    pm = PsumPool(p, banks=5)
    ps_oA = p.psum("ps_oA", [128, 512], F32)
    ps_oB = p.psum("ps_oB", [128, 512], F32)
    ps_s = p.psum("ps_s", [128, 512], F32)
    seg_L3a_local(p, pm, lambda n, s, d=F32: p.din(n, s, d), ps_oA, ps_oB, ps_s,
                  ufo, oaug, qp_o, st_o)
    return p


def seg_L3a_out(p, pm, co, din, ps_b, xin, xout, qp, oaug, st_in, hfc=None):
    nc = p.nc
    sel_a, sel_b, ones_row = co.sel_a, co.sel_b, co.ones_row
    wot = din("wot", [8, 128, 8 * 128])
    bor = din("bor", [128, 8])
    xs = xin
    seg_cm = p.seg()
    seg_cm.__enter__()
    X = [p.sb("X", [128, T], F32) for _ in range(NCH)]
    QP = [p.sb("QP", [128, T], BF16) for _ in range(NCH)]
    O = [p.sb("Ofm", [128, T], BF16) for _ in range(NCH)]
    OAi = [p.sb("OAi", [65, 512], BF16) for _ in range(2)]
    stc = p.sb("stc", [128, 8 * 130], BF16)
    osc = p.sb("osc", [65, 512], F32)
    dn = p.sb("dn", [128, 512], F32)
    Wo_b = [p.sb("Wo_b", [128, 8 * 128], BF16) for _ in range(2)]
    Yt = [p.sb("Yt", [128, 512], F32) for _ in range(NCH)]
    bo_sb = p.sb("bo_sb", [128, 8], F32)

    for c in range(NCH):
        p.load(X[c][:], xs[128 * c:128 * (c + 1), :])
        p.load(QP[c][:], qp[128 * c:128 * (c + 1), :])
    p.load(stc[:], st_in[:, :])
    p.load(bo_sb[:], bor[:, :])
    for e in (nc.scalar, nc.vector, nc.tensor):
        p.wait(e, "dma")
    if hfc is not None:
        p.inc(nc.vector.tensor_scalar(stc[:], stc[:], hfc, None, ALU.mult), "dve")
        p.wait(nc.tensor, "dve")

    oa_g = {}
    n = [0]
    for h in range(16):
        ch, hi, hb = h // 2, h % 2, 64 * (h % 2)
        for g in range(4):
            buf = OAi[n[0] % 2]
            if n[0] >= 2:
                p.wait(nc.sync, "dve", oa_g[n[0] % 2])
            p.load(buf[:], oaug[h, :, 512 * g:512 * (g + 1)])
            bank, pv = pm.acquire(nc.tensor, rows=65)
            p.wait(nc.tensor, "dma")
            p.inc(nc.tensor.matmul(pv,
                                   stc[64 * hi:64 * (hi + 1),
                                       130 * ch + 65 * hi:130 * ch + 65 * (hi + 1)],
                                   QP[ch][hb:hb + 64, 512 * g:512 * (g + 1)],
                                   start=True, stop=True), "pe")
            p.wait(nc.vector, "pe")
            p.inc(nc.vector.tensor_tensor(osc[:], pv, buf[:], ALU.add), "dve")
            pm.release(bank, "dve")
            oa_g[n[0] % 2] = p.count("dve")
            n[0] += 1
            p.wait(nc.vector, "dve")
            p.inc(nc.vector.tensor_scalar(dn[64:65, 0:512], osc[64:65, :], 1e-6, None,
                                          ALU.add), "dve")
            p.wait(nc.scalar, "dve")
            p.inc(nc.scalar.activation(out=dn[64:65, 0:512], in_=dn[64:65, 0:512],
                                       func=AF.Ln), "act")
            p.inc(nc.scalar.activation(out=dn[64:65, 0:512], in_=dn[64:65, 0:512],
                                       func=AF.Exp, scale=-1.0), "act")
            p.wait(nc.tensor, "dve")
            p.wait(nc.tensor, "act")
            i = nc.tensor.matmul(ps_b[0:64, :], co.ones[64:65, 0:64],
                                 dn[64:65, 0:512], start=True, stop=True)
            p.inc(i, "pe")
            p.wait(nc.scalar, "pe")
            p.inc(nc.scalar.copy(dn[0:64, 0:512], ps_b[0:64, :]), "act")
            p.wait(nc.vector, "act")
            p.inc(nc.vector.tensor_tensor(O[ch][hb:hb + 64, 512 * g:512 * (g + 1)],
                                          osc[0:64, :], dn[0:64, 0:512], ALU.mult),
                  "dve")

    stp = {}
    for ts in range(4):
        sl = slice(512 * ts, 512 * (ts + 1))

        def evac(oc, s, pv):
            p.wait(nc.scalar, "pe")
            p.inc(nc.scalar.activation(out=Yt[oc][:], in_=pv, func=AF.Identity,
                                       bias=bo_sb[:, oc:oc + 1], scale=1.0), "act")
            return "act"

        p.wait(nc.tensor, "dve")
        emit_proj_stream(p, pm, wot, 8, 8, 128,
                         lambda kc, s, _sl=sl: O[kc][:, _sl], 512, evac, Wo_b, "wo", stp)
        p.wait(nc.vector, "act")
        emit_ln_fm(p, pm, lambda c: Yt[c][:], 512, sel_a, sel_b, ones_row,
                   res_fn=lambda c, _sl=sl: X[c][:, _sl])

    p.wait(nc.sync, "dve")
    for c in range(NCH):
        p.store(xout[128 * c:128 * (c + 1), :], X[c][:])
    p.barrier()
    seg_cm.__exit__(None, None, None)


def build_L3a_out():
    p = Prog()
    xs = p.din("xs", [D, T])
    qp = p.din("qp", [D, T], BF16)
    oaug = p.din("oaug", [16, 65, T], BF16)
    st_in = p.din("st_in", [128, 8 * 130], BF16)
    ys = p.dout("ys", [D, T])
    p.ln_tmx = 512
    p.start()
    csb, sel_a, sel_b, eye, ones_row, co = load_consts(p)
    pm = PsumPool(p, banks=6)
    ps_b = p.psum("ps_b", [128, 512], F32)
    seg_L3a_out(p, pm, co, lambda n, s, d=F32: p.din(n, s, d), ps_b, xs, ys,
                qp, oaug, st_in)
    return p


def run_L3_attn(pU, pA, pB, x_t, inp):
    shards = shard_stream(x_t)
    cst = host_consts()
    resU = pU.run([{"xs": shards[c], "consts": cst} for c in range(NCORE)])
    ufo = [np.ascontiguousarray(resU.results[c]["us"]) for c in range(NCORE)]
    wqkvt = tile_w(np.asarray(inp["L3_attn_w_qkv"], np.float32))
    bqkr = col128(np.asarray(inp["L3_attn_b_qkv"], np.float32)[0:2048])
    bvrow = np.ascontiguousarray(
        np.asarray(inp["L3_attn_b_qkv"], np.float32)[2048:3072].reshape(1, D))
    wvn = np.ascontiguousarray(np.asarray(inp["L3_attn_w_qkv"], np.float32)[:, 2048:3072])
    proj = np.asarray(inp["L3_attn_proj"], np.float32)
    projp = np.zeros((128, 8 * 64), np.float32)
    for c in range(NCH):
        projp[0:64, 64 * c:64 * (c + 1)] = proj[2 * c]
        projp[64:128, 64 * c:64 * (c + 1)] = proj[2 * c + 1]
    mup = np.triu(np.ones((128, 128), np.float32))
    a_in = {"wqkvt": wqkvt, "bqkr": bqkr, "bvrow": bvrow, "wvn": wvn,
            "projp": projp, "mup": mup}
    resA = pA.run([dict(a_in, ufo=ufo[c]) for c in range(NCORE)])
    wot = tile_w(inp["L3_attn_w_out"])
    bor = col128(inp["L3_attn_b_out"])
    import ml_dtypes
    zst = np.zeros((128, 8 * 130), ml_dtypes.bfloat16)
    in_maps = []
    for c in range(NCORE):
        st_in = zst if c % 2 == 0 else np.ascontiguousarray(resA.results[c - 1]["st_o"])
        in_maps.append({"xs": shards[c], "qp": resA.results[c]["qp_o"],
                        "oaug": resA.results[c]["oaug"], "st_in": st_in,
                        "wot": wot, "bor": bor, "consts": cst})
    res = pB.run(in_maps)
    return gather_stream(res.results)


LOGK_L3 = float(np.log(1152.0))
LNN_L3 = float(np.log(4096.0))


def seg_L3_ffn(p, pm, co, din, ps_b, ps_z, xin, xout, ufo, uf=None,
               s_ag=None, debug=False):
    nc = p.nc
    sel_a, sel_b, ones_row = co.sel_a, co.sel_b, co.ones_row
    tokr = din("tokr", [128, 8])
    hf = din("hf", [1, 1])
    glr = din("glr", [128, 8])
    ghr = din("ghr", [128, 8])
    w1lt = din("w1lt", [4, 128, 8 * 128])
    b1lr = din("b1lr", [128, 4])
    w2lt = din("w2lt", [8, 128, 4 * 128])
    b2lr = din("b2lr", [128, 8])
    w1ht = din("w1ht", [32, 128, 8 * 128])
    b1hr = din("b1hr", [128, 32])
    w2ht = din("w2ht", [8, 128, 32 * 128])
    b2hr = din("b2hr", [128, 8])
    xs, ys = xin, xout
    if debug:
        dbg_m = p.dout("dbg_m", [1, T])
        dbg_s = p.dout("dbg_s", [1, T])
        dbg_sc = p.dout("dbg_sc", [1, 16])
        dbg_s128 = p.dout("dbg_s128", [128, 32])
    seg_cm = p.seg()
    seg_cm.__enter__()
    U = [p.sb("U", [128, T], BF16) for _ in range(NCH)]
    BS = p.sb("BS", [128, 16384], BF16)
    H = [p.sb("H", [128, T], BF16) for _ in range(NCH)]
    YL = [p.sb("YL", [128, 512], BF16) for _ in range(2)]
    yl_dram = p.dint("yl_sc", [D, T], BF16)
    Yt = [p.sb("Yt", [128, 512], F32) for _ in range(NCH)]
    Xs = [p.sb("Xs", [128, 512], F32) for _ in range(NCH)]
    _w1lb = p.sb("W1lb", [128, 8 * 128], BF16)
    W1lb = [_w1lb, _w1lb]
    W2lb = [p.sb("W2lb", [128, 4 * 128], BF16) for _ in range(2)]
    W1hb = [p.sb("W1hb", [128, 8 * 128], BF16) for _ in range(2)]
    W2hb = p.sb("W2hb", [128, 32 * 128], BF16)
    tokb = p.sb("tokb", [128, 8], BF16)
    hfs = p.sb("hfs", [1, 1], F32)
    gl_sb = p.sb("gl_sb", [128, 8], F32)
    gh_sb = p.sb("gh_sb", [128, 8], F32)
    b1l_sb = p.sb("b1l_sb", [128, 4], F32)
    b2l_sb = p.sb("b2l_sb", [128, 8], F32)
    b1h_sb = p.sb("b1h_sb", [128, 32], F32)
    b2h_sb = p.sb("b2h_sb", [128, 8], F32)
    s_row = p.sb("s_row", [1, S], F32)
    s_or = p.sb("s_or", [1, T], F32)
    s128 = p.sb("s128", [128, 32], F32)
    w128 = p.sb("w128", [128, 32], F32)
    e128 = p.sb("e128", [128, 32], F32)
    rsum = p.sb("rsum", [128, 1], F32)
    acol = p.sb("acol", [128, 1], F32)
    sc = p.sb("sc", [1, 16], F32)
    rr = s_row[:, 0:T]
    mrow = s_row[:, T:2 * T]
    ones_r = None
    rstd_r = p.scratch("ln_rss", [1, getattr(p, "ln_tmx", 2624)], F32)[:, 0:T]
    mask_bc = p.sb("mask_bc", [128, T], BF16)

    for c in range(NCH):
        p.load(U[c][:], ufo[128 * c:128 * (c + 1), :])
    p.loadc(tokb[:], tokr[:, :])
    p.load(hfs[:], hf[:, :])
    p.load(gl_sb[:], glr[:, :])
    p.load(gh_sb[:], ghr[:, :])
    p.load(b1l_sb[:], b1lr[:, :])
    p.load(b2l_sb[:], b2lr[:, :])
    p.load(b1h_sb[:], b1hr[:, :])
    p.load(b2h_sb[:], b2hr[:, :])
    for e in (nc.scalar, nc.vector, nc.tensor, nc.gpsimd):
        p.wait(e, "dma")
        p.wait(e, "dve")

    banks = [pm.acquire(nc.tensor, rows=1) for _ in range(4)]
    for kc in range(NCH):
        for s in range(4):
            i = nc.tensor.matmul(banks[s][1], tokb[:, kc:kc + 1],
                                 U[kc][:, 512 * s:512 * (s + 1)],
                                 start=(kc == 0), stop=(kc == NCH - 1))
        p.inc(i, "pe")
    p.wait(nc.vector, "pe")
    for s in range(4):
        p.inc(nc.vector.tensor_copy(s_or[:, 512 * s:512 * (s + 1)], banks[s][1]), "dve")
        pm.release(banks[s][0], "dve")
    if s_ag is not None:
        s_all = s_ag(s_or[:])
        p.wait(nc.sync, "cc")
        p.load(s_row[:, 0:T], s_all[0, :, :])
        p.load(s_row[:, T:S], s_all[1, :, :])
        p.wait(nc.sync, "dma")
    else:
        for ph in range(2):
            banks = [pm.acquire(nc.tensor, rows=1) for _ in range(4)]
            stg = [BS[:, 0:2048], BS[:, 2048:4096]]
            for kc in range(NCH):
                p.wait(nc.sync, "pe")
                p.load(stg[kc % 2][:], uf[128 * kc:128 * (kc + 1),
                                          2048 * ph:2048 * (ph + 1)])
                p.wait(nc.tensor, "dma")
                for s in range(4):
                    i = nc.tensor.matmul(banks[s][1], tokb[:, kc:kc + 1],
                                         stg[kc % 2][:, 512 * s:512 * (s + 1)],
                                         start=(kc == 0), stop=(kc == NCH - 1))
                p.inc(i, "pe")
            p.wait(nc.vector, "pe")
            for s in range(4):
                p.inc(nc.vector.tensor_copy(s_row[:, 2048 * ph + 512 * s:
                                                   2048 * ph + 512 * (s + 1)],
                                            banks[s][1]), "dve")
                pm.release(banks[s][0], "dve")
    s_dram = p.dint(p.uid("sd"), [1, S])
    p.wait(nc.sync, "dve")
    p.store(s_dram[:, :], s_row[:], sem="dma")
    p.wait(nc.sync, "dma")
    p.load(s128[:], s_dram.rearrange("o (p f) -> (o p) f", f=32))

    emit_ln_fm(p, pm, lambda c: U[c][:], T, sel_a, sel_b, ones_row,
               out_fn=lambda c: H[c][:], gamma=lambda c: gl_sb[:, c:c + 1],
               beta=None, mode="rms")

    stw = {}
    G1l = [BS[:, 8192 + 512 * j:8192 + 512 * (j + 1)] for j in range(4)]
    for ts in range(4):
        tsl = slice(512 * ts, 512 * (ts + 1))

        def evl1(mc, s, pv):
            p.wait(nc.scalar, "pe")
            p.inc(nc.scalar.activation(out=G1l[mc], in_=pv, func=AF.Gelu,
                                       bias=b1l_sb[:, mc:mc + 1], scale=1.0), "act")
            return "act"

        emit_proj_stream(p, pm, w1lt, 4, 8, 128,
                         lambda kc, s, _t=tsl: H[kc][:, _t], 512, evl1, W1lb, "w1l", stw)
        p.wait(nc.tensor, "act")

        yst = {"n": 0, "g": {}}

        def evl2(oc, s, pv, _t=tsl, _ts=ts):
            n = yst["n"]; yst["n"] += 1
            buf = YL[n % 2]
            p.wait(nc.scalar, "pe")
            if n >= 2:
                p.wait(nc.scalar, "sto", yst["g"][n % 2])
            p.inc(nc.scalar.activation(out=buf[:], in_=pv, func=AF.Identity,
                                       bias=b2l_sb[:, oc:oc + 1], scale=1.0), "act")
            p.wait(nc.sync, "act")
            yst["g"][n % 2] = p.store(
                yl_dram[128 * oc:128 * (oc + 1), 512 * _ts:512 * (_ts + 1)], buf[:])
            return "act"

        emit_proj_stream(p, pm, w2lt, 8, 4, 128,
                         lambda kc, s: G1l[kc], 512, evl2, W2lb, "w2l", stw)

    p.inc(nc.vector.memset(sc[:], 0.0), "dve")
    p.wait(nc.vector, "dve")
    p.inc(nc.vector.tensor_scalar(sc[:, 0:1], sc[:, 0:1], LOGK_L3 - LNN_L3, None,
                                  ALU.add), "dve")
    A_, Z_, LNZ, LO, HI, TH, GAM, CNT, IA, T1, T2, H0, BASE = range(13)
    for it in range(49):
        p.wait(nc.tensor, "dve")
        p.wait(nc.tensor, "act")
        p.inc(nc.tensor.matmul(ps_b[:, 0:1], co.ones[0:1, :], sc[:, A_:A_ + 1],
                               start=True, stop=True), "pe")
        p.wait(nc.scalar, "pe")
        p.inc(nc.scalar.copy(acol[:], ps_b[:, 0:1]), "act")
        p.wait(nc.vector, "act")
        p.inc(nc.vector.tensor_scalar(w128[:], s128[:], acol[:], 0.0,
                                      ALU.add, ALU.min), "dve")
        p.wait(nc.scalar, "dve")
        p.inc(nc.scalar.activation(out=e128[:], in_=w128[:], func=AF.Exp), "act")
        p.wait(nc.vector, "act")
        p.inc(nc.vector.tensor_reduce(rsum[:], e128[:], mybir.AxisListType.X, ALU.add),
              "dve")
        p.wait(nc.gpsimd, "dve")
        p.inc(nc.gpsimd.tensor_reduce(sc[:, Z_:Z_ + 1], rsum[:], mybir.AxisListType.C,
                                      ALU.add), "pool")
        p.wait(nc.scalar, "pool")
        p.inc(nc.scalar.activation(out=sc[:, LNZ:LNZ + 1], in_=sc[:, Z_:Z_ + 1],
                                   func=AF.Ln), "act")
        p.wait(nc.vector, "act")
        p.inc(nc.vector.tensor_scalar(sc[:, T1:T1 + 1], sc[:, A_:A_ + 1], LOGK_L3, None,
                                      ALU.add), "dve")
        p.wait(nc.vector, "dve")
        p.inc(nc.vector.scalar_tensor_tensor(sc[:, A_:A_ + 1], sc[:, LNZ:LNZ + 1], -1.0,
                                             sc[:, T1:T1 + 1], ALU.mult, ALU.add), "dve")
        p.wait(nc.vector, "dve")
        if debug and it == 0:
            p.wait(nc.vector, "dve")
            p.inc(nc.vector.tensor_copy(sc[:, 14:15], sc[:, A_:A_ + 1]), "dve")
            p.inc(nc.vector.tensor_copy(sc[:, 13:14], sc[:, Z_:Z_ + 1]), "dve")
    p.wait(nc.tensor, "dve")
    p.inc(nc.tensor.matmul(ps_b[:, 0:1], co.ones[0:1, :], sc[:, A_:A_ + 1],
                           start=True, stop=True), "pe")
    p.wait(nc.scalar, "pe")
    p.inc(nc.scalar.copy(acol[:], ps_b[:, 0:1]), "act")
    p.wait(nc.vector, "act")
    p.inc(nc.vector.tensor_scalar(w128[:], s128[:], acol[:], None, ALU.add), "dve")
    p.wait(nc.vector, "dve")
    p.inc(nc.vector.tensor_scalar(e128[:], w128[:], 0.0, None, ALU.is_ge), "dve")
    p.wait(nc.vector, "dve")
    p.inc(nc.vector.tensor_reduce(rsum[:], e128[:], mybir.AxisListType.X, ALU.add), "dve")
    p.wait(nc.gpsimd, "dve")
    p.inc(nc.gpsimd.tensor_reduce(sc[:, CNT:CNT + 1], rsum[:], mybir.AxisListType.C,
                                  ALU.add), "pool")
    p.inc(nc.gpsimd.tensor_reduce(sc[:, H0:H0 + 1], rsum[0:64, :], mybir.AxisListType.C,
                                  ALU.add), "pool")
    p.wait(nc.vector, "pool")
    p.inc(nc.vector.tensor_scalar(sc[:, IA:IA + 1], sc[:, CNT:CNT + 1], 1024.0, None,
                                  ALU.is_ge), "dve")
    p.wait(nc.vector, "dve")
    p.inc(nc.vector.tensor_reduce(rsum[:], s128[:], mybir.AxisListType.X, ALU.max), "dve")
    p.wait(nc.gpsimd, "dve")
    p.inc(nc.gpsimd.tensor_reduce(sc[:, HI:HI + 1], rsum[:], mybir.AxisListType.C,
                                  ALU.max), "pool")
    p.wait(nc.vector, "pool")
    p.inc(nc.vector.tensor_reduce(rsum[:], s128[:], mybir.AxisListType.X, ALU.min), "dve")
    p.wait(nc.vector, "dve")
    p.inc(nc.vector.tensor_scalar(rsum[:], rsum[:], -1.0, None, ALU.mult), "dve")
    p.wait(nc.gpsimd, "dve")
    p.inc(nc.gpsimd.tensor_reduce(sc[:, LO:LO + 1], rsum[:], mybir.AxisListType.C,
                                  ALU.max), "pool")
    p.wait(nc.vector, "pool")
    p.inc(nc.vector.tensor_scalar(sc[:, LO:LO + 1], sc[:, LO:LO + 1], -1.0, None,
                                  ALU.mult), "dve")
    p.wait(nc.vector, "dve")
    p.inc(nc.vector.tensor_scalar(sc[:, HI:HI + 1], sc[:, HI:HI + 1], 1.0, None, ALU.add),
          "dve")
    p.inc(nc.vector.tensor_scalar(sc[:, LO:LO + 1], sc[:, LO:LO + 1], -1.0, None,
                                  ALU.add), "dve")
    for it in range(45):
        p.wait(nc.vector, "pool")
        p.wait(nc.vector, "dve")
        p.inc(nc.vector.tensor_tensor(sc[:, TH:TH + 1], sc[:, LO:LO + 1],
                                      sc[:, HI:HI + 1], ALU.add), "dve")
        p.wait(nc.vector, "dve")
        p.inc(nc.vector.tensor_scalar(sc[:, TH:TH + 1], sc[:, TH:TH + 1], 0.5, None,
                                      ALU.mult), "dve")
        p.wait(nc.vector, "dve")
        p.wait(nc.tensor, "dve")
        p.wait(nc.tensor, "act")
        p.inc(nc.tensor.matmul(ps_b[:, 0:1], co.ones[0:1, :], sc[:, TH:TH + 1],
                               start=True, stop=True), "pe")
        p.wait(nc.scalar, "pe")
        p.inc(nc.scalar.copy(acol[:], ps_b[:, 0:1]), "act")
        p.wait(nc.vector, "act")
        p.inc(nc.vector.tensor_scalar(e128[:], s128[:], acol[:], None, ALU.is_gt), "dve")
        p.wait(nc.vector, "dve")
        p.inc(nc.vector.tensor_reduce(rsum[:], e128[:], mybir.AxisListType.X, ALU.add),
              "dve")
        p.wait(nc.gpsimd, "dve")
        p.inc(nc.gpsimd.tensor_reduce(sc[:, CNT:CNT + 1], rsum[:], mybir.AxisListType.C,
                                      ALU.add), "pool")
        p.wait(nc.vector, "pool")
        p.inc(nc.vector.tensor_scalar(sc[:, GAM:GAM + 1], sc[:, CNT:CNT + 1], 1024.0,
                                      None, ALU.is_ge), "dve")
        p.wait(nc.vector, "dve")
        p.inc(nc.vector.tensor_scalar(sc[:, T2:T2 + 1], sc[:, GAM:GAM + 1], -1.0, 1.0,
                                      ALU.mult, ALU.add), "dve")
        p.wait(nc.vector, "dve")
        p.inc(nc.vector.tensor_tensor(sc[:, T1:T1 + 1], sc[:, GAM:GAM + 1],
                                      sc[:, TH:TH + 1], ALU.mult), "dve")
        p.inc(nc.vector.tensor_tensor(sc[:, LO:LO + 1], sc[:, T2:T2 + 1],
                                      sc[:, LO:LO + 1], ALU.mult), "dve")
        p.wait(nc.vector, "dve")
        p.inc(nc.vector.tensor_tensor(sc[:, LO:LO + 1], sc[:, LO:LO + 1],
                                      sc[:, T1:T1 + 1], ALU.add), "dve")
        p.inc(nc.vector.tensor_tensor(sc[:, T1:T1 + 1], sc[:, T2:T2 + 1],
                                      sc[:, TH:TH + 1], ALU.mult), "dve")
        p.inc(nc.vector.tensor_tensor(sc[:, HI:HI + 1], sc[:, GAM:GAM + 1],
                                      sc[:, HI:HI + 1], ALU.mult), "dve")
        p.wait(nc.vector, "dve")
        p.inc(nc.vector.tensor_tensor(sc[:, HI:HI + 1], sc[:, HI:HI + 1],
                                      sc[:, T1:T1 + 1], ALU.add), "dve")
    ones_r = p.scratch("ln_rt", [1, getattr(p, "ln_tmx", 2624)], F32)[:, 0:T]
    p.inc(nc.vector.memset(ones_r[:], 1.0), "dve")
    p.inc(nc.vector.tensor_tensor(sc[:, BASE:BASE + 1], sc[:, H0:H0 + 1], hfs[:],
                                  ALU.mult), "dve")
    p.wait(nc.vector, "dve")
    p.inc(nc.vector.tensor_scalar(rr[:], s_or[:], sc[:, A_:A_ + 1], 0.0,
                                  ALU.add, ALU.is_ge), "dve")
    p.wait(nc.vector, "dve")
    p.inc(nc.vector.tensor_tensor_scan(mrow[:], ones_r[:], rr[:],
                                       sc[:, BASE:BASE + 1], ALU.mult, ALU.add), "dve")
    p.wait(nc.vector, "dve")
    p.inc(nc.vector.tensor_scalar(mrow[:], mrow[:], 1024.0, None, ALU.is_le), "dve")
    p.wait(nc.vector, "dve")
    p.inc(nc.vector.tensor_tensor(rr[:], rr[:], mrow[:], ALU.mult), "dve")
    p.inc(nc.vector.tensor_scalar(s_or[:], s_or[:], sc[:, LO:LO + 1], None, ALU.is_gt),
          "dve")
    p.wait(nc.vector, "dve")
    p.inc(nc.vector.tensor_scalar(rr[:], rr[:], sc[:, IA:IA + 1], None, ALU.mult),
          "dve")
    p.inc(nc.vector.tensor_scalar(sc[:, T2:T2 + 1], sc[:, IA:IA + 1], -1.0, 1.0,
                                  ALU.mult, ALU.add), "dve")
    p.wait(nc.vector, "dve")
    p.wait(nc.vector, "pool")
    p.inc(nc.vector.tensor_scalar(mrow[:], s_or[:], sc[:, T2:T2 + 1], None, ALU.mult),
          "dve")
    p.wait(nc.vector, "dve")
    p.inc(nc.vector.tensor_tensor(mrow[:], mrow[:], rr[:], ALU.add), "dve")
    if debug:
        p.wait(nc.sync, "dve")
        p.store(dbg_s128[:, :], s128[:])
        p.store(dbg_m[:, :], mrow[:])
        p.store(dbg_sc[:, :], sc[:])
        p.store(dbg_s[:, :], s_or[:])
    for s in range(4):
        p.wait(nc.tensor, "dve")
        p.wait(nc.tensor, "act")
        p.inc(nc.tensor.matmul(ps_b[:, 0:512], co.ones[0:1, :],
                               mrow[:, 512 * s:512 * (s + 1)], start=True, stop=True),
              "pe")
        p.wait(nc.scalar, "pe")
        p.inc(nc.scalar.copy(mask_bc[:, 512 * s:512 * (s + 1)], ps_b[:, 0:512]), "act")

    for s in range(4):
        p.wait(nc.tensor, "act")
        p.wait(nc.tensor, "dve")
        p.inc(nc.tensor.matmul(ps_z[:, 0:512], co.ones[0:1, :],
                               rstd_r[:, 512 * s:512 * (s + 1)], start=True, stop=True),
              "pe")
        p.wait(nc.vector, "pe")
        for c in range(NCH):
            p.inc(nc.vector.scalar_tensor_tensor(
                H[c][:, 512 * s:512 * (s + 1)], U[c][:, 512 * s:512 * (s + 1)],
                gh_sb[:, c:c + 1], ps_z[:, 0:512], ALU.mult, ALU.mult), "dve")

    G1h = [BS[:, 512 * j:512 * (j + 1)] for j in range(32)]
    for ts in range(4):
        tsl = slice(512 * ts, 512 * (ts + 1))

        def evh1(mc, s, pv):
            p.wait(nc.scalar, "pe")
            p.inc(nc.scalar.activation(out=G1h[mc], in_=pv, func=AF.Gelu,
                                       bias=b1h_sb[:, mc:mc + 1], scale=1.0), "act")
            return "act"

        p.wait(nc.tensor, "dve")
        emit_proj_stream(p, pm, w1ht, 32, 8, 128,
                         lambda kc, s, _t=tsl: H[kc][:, _t], 512, evh1, W1hb, "w1h", stw)
        p.wait(nc.tensor, "act")

        def evh2(oc, s, pv, _t=tsl, _ts=ts):
            buf = YL[oc % 2]
            p.wait(nc.sync, "dve")
            p.load(buf[:], yl_dram[128 * oc:128 * (oc + 1),
                                   512 * _ts:512 * (_ts + 1)], sem="yl")
            p.wait(nc.vector, "pe")
            p.inc(nc.vector.scalar_tensor_tensor(
                Yt[oc][:], pv, b2h_sb[:, oc:oc + 1], mask_bc[:, _t],
                ALU.add, ALU.mult), "dve")
            p.wait(nc.vector, "yl")
            p.wait(nc.vector, "dve")
            p.inc(nc.vector.tensor_tensor(Yt[oc][:], Yt[oc][:], buf[:], ALU.add),
                  "dve")
            return "dve"

        emit_proj_stream(p, pm, w2ht, 8, 32, 128,
                         lambda kc, s: G1h[kc], 512, evh2, [W2hb, W2hb], "w2h", stw)
        for c in range(NCH):
            p.load(Xs[c][:], xs[128 * c:128 * (c + 1), tsl])
        p.wait(nc.vector, "dma")
        p.wait(nc.scalar, "dma")
        emit_ln_fm(p, pm, lambda c: Yt[c][:], 512, sel_a, sel_b, ones_row,
                   res_fn=lambda c: Xs[c][:])
        p.wait(nc.sync, "dve")
        for c in range(NCH):
            p.store(ys[128 * c:128 * (c + 1), tsl], Xs[c][:])
    p.barrier()
    seg_cm.__exit__(None, None, None)


def build_L3_ffn(debug=False):
    p = Prog()
    xs = p.din("xs", [D, T])
    ufo = p.din("ufo", [D, T], BF16)
    uf = p.din("uf", [D, S], BF16)
    ys = p.dout("ys", [D, T])
    p.ln_tmx = 2048
    p.start()
    csb, sel_a, sel_b, eye, ones_row, co = load_consts(p)
    pm = PsumPool(p, banks=6)
    ps_b = p.psum("ps_b", [128, 512], F32)
    ps_z = p.psum("ps_z", [128, 512], F32)
    seg_L3_ffn(p, pm, co, lambda n, s, d=F32: p.din(n, s, d), ps_b, ps_z,
               xs, ys, ufo, uf=uf, debug=debug)
    return p


def run_L3_ffn(pU, pG, x_t, inp):
    shards = shard_stream(x_t)
    cst = host_consts()
    resU = pU.run([{"xs": shards[c], "consts": cst} for c in range(NCORE)])
    import ml_dtypes
    u_t = np.empty((B, D, S), ml_dtypes.bfloat16)
    for c in range(NCORE):
        b, sl = core_slice(c)
        u_t[b, :, sl] = resU.results[c]["us"]
    g_in = {
        "tokr": col128(inp["L3_ffn_routing_token"]),
        "glr": col128(inp["L3_ffn_gamma_l"]), "ghr": col128(inp["L3_ffn_gamma_h"]),
        "w1lt": tile_w(np.ascontiguousarray(np.asarray(inp["L3_ffn_w1_l"], np.float32).T)),
        "b1lr": col128(inp["L3_ffn_b1_l"]),
        "w2lt": tile_w(np.ascontiguousarray(np.asarray(inp["L3_ffn_w2_l"], np.float32).T)),
        "b2lr": col128(inp["L3_ffn_b2_l"]),
        "w1ht": tile_w(np.ascontiguousarray(np.asarray(inp["L3_ffn_w1_h"], np.float32).T)),
        "b1hr": col128(inp["L3_ffn_b1_h"]),
        "w2ht": tile_w(np.ascontiguousarray(np.asarray(inp["L3_ffn_w2_h"], np.float32).T)),
        "b2hr": col128(inp["L3_ffn_b2_h"]),
        "consts": cst,
    }
    in_maps = []
    for c in range(NCORE):
        b, sl = core_slice(c)
        in_maps.append(dict(
            g_in, xs=shards[c],
            ufo=np.ascontiguousarray(u_t[b, :, sl]),
            uf=np.ascontiguousarray(u_t[b]),
            hf=np.full((1, 1), float(c % 2), np.float32)))
    res = pG.run(in_maps)
    return gather_stream(res.results)


def qk_perm(inner):
    perm = []
    for ch in range(inner // 128):
        base = 128 * ch
        perm += list(range(base, base + 32))
        perm += list(range(base + 64, base + 96))
        perm += list(range(base + 32, base + 64))
        perm += list(range(base + 96, base + 128))
    return np.asarray(perm)


def rot_tables(window, t0, t1, side):
    half = np.arange(0, 64, 2, np.float32)
    inv_freq = 1.0 / 10000.0 ** (half / 64.0)
    t = np.arange(t0, t1, dtype=np.float32)
    freqs = t[:, None] * inv_freq[None, :]
    base = (half + 0.4 * 64.0) / (1.4 * 64.0)
    power = (t - 4096 // 2) / float(window // 2)
    scale = base[None, :] ** power[:, None]
    if side == "k":
        scale = 1.0 / scale
    cos = np.cos(freqs) * scale
    sin = np.sin(freqs) * scale
    CS = np.empty((128, t1 - t0), np.float32)
    SS = np.empty((128, t1 - t0), np.float32)
    for r0 in (0, 32, 64, 96):
        CS[r0:r0 + 32] = cos.T
        SS[r0:r0 + 32] = sin.T
    SS[0:32] *= -1.0
    SS[64:96] *= -1.0
    return CS, SS


def host_gates(W, nkt):
    g = np.full((128, 256), -30000.0, np.float32)
    iu = np.triu_indices(128)
    g[:, 0:128][np.tril_indices(128)] = -30000.0
    gu = np.where(np.arange(128)[None, :] >= np.arange(128)[:, None], 0.0, -30000.0)
    gl = np.where(np.arange(128)[None, :] <= np.arange(128)[:, None], 0.0, -30000.0)
    return np.concatenate([gu, gl], axis=1).astype(np.float32)


def seg_C_attn(p, pm, co, din, ps_o, ps_b, zin, zout, zh, fam, hfc=None,
               debug=False):
    W, HEADS, INNER = fam["W"], fam["HEADS"], fam["INNER"]
    NCI = INNER // 128
    NQB = INNER // 128
    HB = W + 1
    NVT = (T + W) // 128
    NKT = W // 128 + 1
    NW = T // W
    NJ = W // 128
    nc = p.nc
    csb = co.csb
    zs, ys = zin, zout
    wqkvt = din("wqkvt", [2 * NQB, 128, 8 * 128])
    wvn = din("wvn", [D, INNER])
    wot = din("wot", [8, 128, NCI * 128])
    qsr = din("qsr", [128, 2])
    ksr = din("ksr", [128, 2])
    rotq = din("rotq", [128, 2 * T])
    rotk = din("rotk", [128, 2 * (T + W)])
    padm = din("padm", [128, 1])
    gates = din("gates", [128, 256])
    selpb_in = din("selpb", [33, 128])
    if debug:
        dbg_qp = p.dout("dbg_qp", [128, T], BF16)
        dbg_kp = p.dout("dbg_kp", [128, T + fam["W"]], BF16)
        dbg_o = p.dout("dbg_o", [fam["INNER"] // 128, 128, T], BF16)
    seg_cm = p.seg()
    seg_cm.__enter__()
    selpb = p.sb("selpb", [33, 128], F32)
    p.load(selpb[:], selpb_in[:, :])
    ZS = [p.sb("ZS", [128, T], F32) for _ in range(2)]
    ZH1 = p.sb("ZH1", [128, HB], F32)
    Hc = [p.sb("Hc", [128, T], BF16) for _ in range(NCH)]
    Hh = [p.sb("Hh", [128, W], BF16) for _ in range(NCH)]
    O = [p.sb("Ofm", [128, T], BF16) for _ in range(NCI)]
    Qc = p.sb("Qc", [128, T], BF16)
    Kc = p.sb("Kc", [128, T + W], BF16)
    QR = p.sb("QR", [128, T + W], BF16)
    Vc = p.sb("Vc", [128, NVT * 130], BF16)
    rows33 = p.sb("rows33", [33, T + W], F32)
    u1 = p.sb("u1", [128, 512], F32)
    u2 = p.sb("u2", [128, 512], F32)
    Ab = [p.sb("Ab", [128, NKT * 128], BF16) for _ in range(2)]
    negm = p.sb("negm", [1, 128], F32)
    mcol = p.sb("mcol", [128, 1], F32)
    dn = p.sb("dn", [128, 128], F32)
    tq = p.sb("tq", [128, 2 * T], BF16)
    tk = p.sb("tk", [128, 2 * (T + W)], BF16)
    gat = p.sb("gat", [128, 256], F32)
    pad_sb = p.sb("pad_sb", [128, 1], F32)
    qs_sb = p.sb("qs_sb", [128, 2], F32)
    ks_sb = p.sb("ks_sb", [128, 2], F32)
    Wb = [p.sb("Wb", [128, 8 * 128], BF16) for _ in range(2)]
    Wvb = [p.sb("Wvb", [128, INNER], BF16) for _ in range(NCH)]
    Wo_b = [p.sb("Wo_b", [128, NCI * 128], BF16) for _ in range(2)]

    p.inc(nc.vector.memset(Vc[:], 1.0), "dve")
    for c in range(NCH):
        p.loadc(Wvb[c][:], wvn[128 * c:128 * (c + 1), :])
    p.loadc(tq[:], rotq[:, :])
    p.loadc(tk[:], rotk[:, :])
    p.load(gat[:], gates[:, :])
    p.load(pad_sb[:], padm[:, :])
    p.load(qs_sb[:], qsr[:, :])
    p.load(ks_sb[:], ksr[:, :])
    ldz = {}
    zh_guard = 0
    for c in range(NCH):
        stg = ZS[c % 2]
        if c >= 2:
            p.wait(nc.sync, "dve", ldz[c % 2])
        p.load(stg[:], zs[128 * c:128 * (c + 1), :])
        p.wait(nc.sync, "dve", zh_guard)
        p.load(ZH1[:], zh[128 * c:128 * (c + 1), :])
        p.wait(nc.vector, "dma")
        if hfc is not None:
            p.inc(nc.vector.tensor_scalar(ZH1[:], ZH1[:], hfc, None, ALU.mult), "dve")
            p.wait(nc.vector, "dve")
        if c < 4:
            p.inc(nc.vector.tensor_copy(Hc[c][:], stg[:]), "dve")
            p.inc(nc.vector.tensor_copy(Hh[c][:], ZH1[:, 1:HB]), "dve")
        else:
            p.inc(nc.vector.tensor_copy(Hc[c][:, 1:T], stg[:, 0:T - 1]), "dve")
            p.inc(nc.vector.tensor_copy(Hc[c][:, 0:1], ZH1[:, HB - 1:HB]), "dve")
            p.inc(nc.vector.tensor_copy(Hh[c][:], ZH1[:, 0:HB - 1]), "dve")
        ldz[c % 2] = p.count("dve")
        zh_guard = p.count("dve")
    p.wait(nc.vector, "dma")
    p.inc(nc.vector.tensor_scalar(tq[:, 0:T], tq[:, 0:T], qs_sb[:, 0:1], None,
                                  ALU.mult), "dve")
    p.inc(nc.vector.tensor_scalar(tq[:, T:2 * T], tq[:, T:2 * T], qs_sb[:, 1:2], None,
                                  ALU.mult), "dve")
    p.inc(nc.vector.tensor_scalar(tk[:, 0:T + W], tk[:, 0:T + W], ks_sb[:, 0:1], None,
                                  ALU.mult), "dve")
    p.inc(nc.vector.tensor_scalar(tk[:, T + W:], tk[:, T + W:], ks_sb[:, 1:2], None,
                                  ALU.mult), "dve")
    for e in (nc.scalar, nc.tensor, nc.gpsimd):
        p.wait(e, "dma")
        p.wait(e, "dve")

    wst = {}
    po_free = 0
    for ci in range(NCI):
        for side, blk in (("q", ci), ("k", NQB + ci)):
            buf = Wb[blk % 2]
            if id(buf) in wst:
                p.wait(nc.gpsimd, "pe", wst[id(buf)])
            p.loadc(buf[:], wqkvt[blk, :, :], sem="wdma")
            p.wait(nc.tensor, "wdma")
            nsl = 4 if side == "q" else 5
            for s in range(nsl):
                if side == "q":
                    rhs = [Hc[kc][:, 512 * s:512 * (s + 1)] for kc in range(NCH)]
                    dst = Qc[:, 512 * s:512 * (s + 1)]
                    wcols = 512
                elif s == 0:
                    rhs = [Hh[kc][:] for kc in range(NCH)]
                    dst = Kc[:, 0:W]
                    wcols = W
                else:
                    rhs = [Hc[kc][:, 512 * (s - 1):512 * s] for kc in range(NCH)]
                    dst = Kc[:, W + 512 * (s - 1):W + 512 * s]
                    wcols = 512
                bank, pv = pm.acquire(nc.tensor, cols=wcols)
                for kc in range(NCH):
                    i = nc.tensor.matmul(pv, buf[:, 128 * kc:128 * (kc + 1)], rhs[kc],
                                         start=(kc == 0), stop=(kc == NCH - 1))
                wst[id(buf)] = p.inc(i, "pe")
                p.wait(nc.vector, "pe")
                p.inc(nc.vector.tensor_copy(dst, pv), "dve")
                pm.release(bank, "dve")
        p.wait(nc.tensor, "dve")
        for i_ in range(NVT):
            if i_ < W // 128:
                rhsrc = [Hh[kc][:, 128 * i_:128 * (i_ + 1)] for kc in range(NCH)]
            else:
                io = i_ - W // 128
                rhsrc = [Hc[kc][:, 128 * io:128 * (io + 1)] for kc in range(NCH)]
            bank, pv = pm.acquire(nc.tensor, cols=128)
            for kc in range(NCH):
                i = nc.tensor.matmul(pv, rhsrc[kc], Wvb[kc][:, 128 * ci:128 * (ci + 1)],
                                     start=(kc == 0), stop=(kc == NCH - 1))
            p.inc(i, "pe")
            p.wait(nc.vector, "pe")
            dstv = Vc[:, 130 * i_:130 * (i_ + 1)].rearrange(
                "p (h e) -> p h e", e=65)[:, :, 0:64]
            p.inc(nc.vector.tensor_copy(dstv, pv.rearrange("p (h e) -> p h e", e=64)),
                  "dve")
            pm.release(bank, "dve")
        for side in ("q", "k"):
            src_t = Qc if side == "q" else Kc
            wid = T if side == "q" else T + W
            tbl = tq if side == "q" else tk
            for s in range((wid + 511) // 512):
                sl = slice(512 * s, min(512 * (s + 1), wid))
                wd = sl.stop - sl.start
                p.wait(nc.scalar, "dve")
                p.wait(nc.scalar, "pe")
                p.inc(nc.scalar.activation(out=u1[:, 0:wd], in_=src_t[:, sl],
                                           func=AF.Square), "act")
                bank, pv = pm.acquire(nc.tensor, rows=33, cols=wd)
                p.wait(nc.tensor, "act")
                p.inc(nc.tensor.matmul(pv, co.sel_h, u1[:, 0:wd], start=True, stop=True),
                      "pe")
                p.wait(nc.vector, "pe")
                p.inc(nc.vector.tensor_copy(rows33[0:33, sl], pv), "dve")
                pm.release(bank, "dve")
            p.wait(nc.vector, "dve")
            p.inc(nc.vector.tensor_scalar(rows33[:, 0:wid], rows33[:, 0:wid], 1e-24,
                                          None, ALU.max), "dve")
            p.wait(nc.scalar, "dve")
            p.inc(nc.scalar.activation(out=rows33[:, 0:wid], in_=rows33[:, 0:wid],
                                       func=AF.Ln), "act")
            p.inc(nc.scalar.activation(out=rows33[:, 0:wid], in_=rows33[:, 0:wid],
                                       func=AF.Exp, scale=-0.5), "act")
            p.wait(nc.sync, "dve")
            p.wait(nc.sync, "pe")
            for r0 in (0, 64):
                p.load(QR[r0:r0 + 32, 0:wid], src_t[r0 + 32:r0 + 64, 0:wid], sem="rot")
                p.load(QR[r0 + 32:r0 + 64, 0:wid], src_t[r0:r0 + 32, 0:wid], sem="rot")
            for s in range((wid + 511) // 512):
                sl = slice(512 * s, min(512 * (s + 1), wid))
                wd = sl.stop - sl.start
                p.wait(nc.tensor, "dve")
                p.wait(nc.tensor, "act")
                bank, pv = pm.acquire(nc.tensor, cols=wd)
                p.inc(nc.tensor.matmul(pv, selpb[:, :], rows33[:, sl],
                                       start=True, stop=True), "pe")
                p.wait(nc.vector, "pe")
                p.wait(nc.vector, "rot")
                p.inc(nc.vector.tensor_tensor(u1[:, 0:wd], src_t[:, sl],
                                              tbl[:, sl], ALU.mult), "dve")
                p.inc(nc.vector.tensor_tensor(u2[:, 0:wd], QR[:, sl],
                                              tbl[:, wid + sl.start:wid + sl.stop],
                                              ALU.mult), "dve")
                p.wait(nc.vector, "dve")
                p.inc(nc.vector.tensor_tensor(u1[:, 0:wd], u1[:, 0:wd], u2[:, 0:wd],
                                              ALU.add), "dve")
                p.wait(nc.vector, "dve")
                p.inc(nc.vector.tensor_tensor(src_t[:, sl], u1[:, 0:wd], pv, ALU.mult),
                      "dve")
                pm.release(bank, "dve")
        p.wait(nc.tensor, "dve")
        if debug and ci == 0:
            p.wait(nc.sync, "dve")
            p.store(dbg_qp[:, :], Qc[:])
            p.store(dbg_kp[:, :], Kc[:])
        for hb, hsel in ((0, 0), (64, 1)):
            for w in range(NW):
                for j in range(NJ):
                    qsl = slice(W * w + 128 * j, W * w + 128 * (j + 1))
                    banks, pva = pm.acquire_seq(nc.tensor, (NKT * 128 + 511) // 512)
                    p.wait(nc.tensor, "act")
                    p.wait(nc.tensor, "dve")
                    for t_ in range(NKT):
                        kt = w * (W // 128) + j + t_
                        i = nc.tensor.matmul(
                            pva[:, 128 * t_:128 * (t_ + 1)],
                            Qc[hb:hb + 64, qsl],
                            Kc[hb:hb + 64, 128 * kt:128 * (kt + 1)],
                            start=True, stop=True)
                    p.inc(i, "pe")
                    p.wait(nc.vector, "pe")
                    p.inc(nc.vector.tensor_tensor(pva[:, 0:128], pva[:, 0:128],
                                                  gat[:, 0:128], ALU.add), "dve")
                    p.inc(nc.vector.tensor_tensor(
                        pva[:, 128 * (NKT - 1):128 * NKT],
                        pva[:, 128 * (NKT - 1):128 * NKT], gat[:, 128:256], ALU.add),
                        "dve")
                    if w == 0:
                        for t_ in range(NKT):
                            if j + t_ < W // 128:
                                p.inc(nc.vector.tensor_scalar(
                                    pva[:, 128 * t_:128 * (t_ + 1)],
                                    pva[:, 128 * t_:128 * (t_ + 1)],
                                    pad_sb[:], None, ALU.add), "dve")
                    p.wait(nc.vector, "dve")
                    p.inc(nc.vector.tensor_reduce(mcol[:], pva[:, 0:128 * NKT],
                                                  mybir.AxisListType.X, ALU.max), "dve")
                    for b_ in banks:
                        pm.release(b_, "dve")
                    p.wait(nc.tensor, "dve")
                    p.wait(nc.tensor, "act")
                    i = nc.tensor.transpose(ps_b[0:1, 0:128], mcol[:], csb[:, 35:163])
                    p.inc(i, "pe")
                    p.wait(nc.vector, "pe")
                    p.inc(nc.vector.tensor_scalar(negm[:], ps_b[0:1, 0:128], -1.0, None,
                                                  ALU.mult), "dve")
                    banks, pvb = pm.acquire_seq(nc.tensor, (NKT * 128 + 511) // 512)
                    p.wait(nc.tensor, "act")
                    p.wait(nc.tensor, "dve")
                    for t_ in range(NKT):
                        kt = w * (W // 128) + j + t_
                        nc.tensor.matmul(
                            pvb[:, 128 * t_:128 * (t_ + 1)],
                            Kc[hb:hb + 64, 128 * kt:128 * (kt + 1)],
                            Qc[hb:hb + 64, qsl], start=True, stop=False)
                        i = nc.tensor.matmul(
                            pvb[:, 128 * t_:128 * (t_ + 1)],
                            co.ones[0:1, 0:128], negm[:, :], start=False, stop=True)
                    p.inc(i, "pe")
                    p.wait(nc.vector, "pe")
                    p.inc(nc.vector.tensor_tensor(pvb[:, 0:128], pvb[:, 0:128],
                                                  gat[:, 128:256], ALU.add), "dve")
                    p.inc(nc.vector.tensor_tensor(
                        pvb[:, 128 * (NKT - 1):128 * NKT],
                        pvb[:, 128 * (NKT - 1):128 * NKT], gat[:, 0:128], ALU.add),
                        "dve")
                    if w == 0:
                        for t_ in range(NKT):
                            if j + t_ < W // 128:
                                p.inc(nc.vector.tensor_scalar(
                                    pvb[:, 128 * t_:128 * (t_ + 1)],
                                    pvb[:, 128 * t_:128 * (t_ + 1)],
                                    pad_sb[:], None, ALU.add), "dve")
                    p.wait(nc.scalar, "dve")
                    p.wait(nc.scalar, "pe")
                    p.inc(nc.scalar.activation(out=Ab[(w * NJ + j) % 2][:, 0:128 * NKT],
                                               in_=pvb[:, 0:128 * NKT], func=AF.Exp,
                                               scale=8.0), "act")
                    for b_ in banks:
                        pm.release(b_, "act")
                    p.wait(nc.tensor, "dve", po_free)
                    p.wait(nc.tensor, "act")
                    for t_ in range(NKT):
                        kt = w * (W // 128) + j + t_
                        i = nc.tensor.matmul(
                            ps_o[0:65, 0:128],
                            Vc[:, 130 * kt + 65 * hsel:130 * kt + 65 * (hsel + 1)],
                            Ab[(w * NJ + j) % 2][:, 128 * t_:128 * (t_ + 1)],
                            start=(t_ == 0), stop=(t_ == NKT - 1))
                    p.inc(i, "pe")
                    p.wait(nc.scalar, "pe")
                    p.inc(nc.scalar.activation(out=dn[64:65, 0:128],
                                               in_=ps_o[64:65, 0:128], func=AF.Ln),
                          "act")
                    p.inc(nc.scalar.activation(out=dn[64:65, 0:128],
                                               in_=dn[64:65, 0:128], func=AF.Exp,
                                               scale=-1.0), "act")
                    p.wait(nc.tensor, "dve")
                    p.wait(nc.tensor, "act")
                    i = nc.tensor.matmul(ps_b[0:64, 0:128], co.ones[64:65, 0:64],
                                         dn[64:65, 0:128], start=True, stop=True)
                    p.inc(i, "pe")
                    p.wait(nc.scalar, "pe")
                    p.inc(nc.scalar.copy(dn[0:64, 0:128], ps_b[0:64, 0:128]), "act")
                    p.wait(nc.vector, "act")
                    p.inc(nc.vector.tensor_tensor(O[ci][hb:hb + 64, qsl],
                                                  ps_o[0:64, 0:128], dn[0:64, 0:128],
                                                  ALU.mult), "dve")
                    po_free = p.count("dve")

    if debug:
        p.wait(nc.sync, "dve")
        for ci_ in range(NCI):
            p.store(dbg_o[ci_, :, :], O[ci_][:])
    st2 = {}
    Zt = [p.sb("Zt", [128, 512], F32) for _ in range(2)]
    zt_g = {}
    zn = [0]
    for ts in range(4):
        tsl = slice(512 * ts, 512 * (ts + 1))

        def evo(oc, s, pv, _t=tsl, _ts=ts):
            n = zn[0]
            zn[0] += 1
            buf = Zt[n % 2]
            if n >= 2:
                p.wait(nc.gpsimd, "sto", zt_g[n % 2])
                p.wait(nc.gpsimd, "dve")
            p.load(buf[:], zs[128 * oc:128 * (oc + 1), _t], q=nc.gpsimd, sem="zre")
            p.wait(nc.vector, "zre")
            p.wait(nc.vector, "pe")
            p.inc(nc.vector.tensor_tensor(buf[:], buf[:], pv, ALU.add), "dve")
            p.wait(nc.sync, "dve")
            zt_g[n % 2] = p.store(ys[128 * oc:128 * (oc + 1), _t], buf[:])
            return "dve"

        p.wait(nc.tensor, "dve")
        emit_proj_stream(p, pm, wot, 8, NCI, 128,
                         lambda kc, s, _t=tsl: O[kc][:, _t], 512, evo, Wo_b, "wo", st2)
    p.barrier()
    seg_cm.__exit__(None, None, None)


def build_C_attn(fam, debug=False):
    p = Prog()
    zs = p.din("zs", [D, T])
    zh = p.din("zh", [D, fam["W"] + 1])
    ys = p.dout("ys", [D, T])
    p.start()
    csb, sel_a, sel_b, eye, ones_row, co = load_consts(p)
    pm = PsumPool(p, banks=6)
    ps_o = p.psum("ps_o", [128, 512], F32)
    ps_b = p.psum("ps_b", [128, 512], F32)
    seg_C_attn(p, pm, co, lambda n, s, d=F32: p.din(n, s, d), ps_o, ps_b,
               zs, ys, zh, fam, debug=debug)
    return p


def seg_C_ffn(p, pm, co, din, zin, zout, zh1, fam, last=False, xin=None,
              xout=None, hfc=None):
    nc = p.nc
    sel_a, sel_b, ones_row = co.sel_a, co.sel_b, co.ones_row
    zs = zin
    lngr = din("lngr", [128, 8])
    lnbr = din("lnbr", [128, 8])
    w1t = din("w1t", [44, 128, 8 * 128])
    b1r = din("b1r", [128, 44])
    w2t = din("w2t", [8, 128, 22 * 128])
    b2r = din("b2r", [128, 8])
    xs = xin
    ys = xout if last else zout
    seg_cm = p.seg()
    seg_cm.__enter__()
    Z = [p.sb("Z", [128, T], F32) for _ in range(NCH)]
    HS = p.sb("HSbig", [128, NCH * T], BF16)
    H = [p.sb("H", [128, T], BF16) for _ in range(NCH)]
    zh_sb = p.sb("zh_sb", [128, 8], F32)
    lng_sb = p.sb("lng_sb", [128, 8], F32)
    lnb_sb = p.sb("lnb_sb", [128, 8], F32)
    b1_sb = p.sb("b1_sb", [128, 44], F32)
    b2_sb = p.sb("b2_sb", [128, 8], F32)
    W1b = [p.sb("W1b", [128, 8 * 128], BF16) for _ in range(2)]
    W2b = p.sb("W2b", [128, 22 * 128], BF16)
    gg = p.sb("gg", [128, 512], F32)
    Xs = [p.sb("Xs", [128, 512], F32) for _ in range(NCH)] if last else None

    for c in range(NCH):
        p.load(Z[c][:], zs[128 * c:128 * (c + 1), :])
    p.load(zh_sb[:], zh1[:, :], allow_slow_non_contiguous=True)
    p.load(lng_sb[:], lngr[:, :])
    if hfc is not None:
        p.wait(nc.vector, "dma")
        p.inc(nc.vector.tensor_scalar(zh_sb[:], zh_sb[:], hfc, None, ALU.mult), "dve")
        p.wait(nc.vector, "dve")
    p.load(lnb_sb[:], lnbr[:, :])
    p.load(b1_sb[:], b1r[:, :])
    p.load(b2_sb[:], b2r[:, :])
    p.wait(nc.vector, "dma")
    HSv = [HS[:, T * c:T * (c + 1)] for c in range(NCH)]
    for c in range(NCH):
        if c < 4:
            p.inc(nc.vector.tensor_copy(HSv[c][:, :], Z[c][:]), "dve")
        else:
            p.inc(nc.vector.tensor_copy(HSv[c][:, 1:T], Z[c][:, 0:T - 1]), "dve")
            p.inc(nc.vector.tensor_copy(HSv[c][:, 0:1], zh_sb[:, c:c + 1]), "dve")
    for e in (nc.scalar, nc.tensor):
        p.wait(e, "dma")
        p.wait(e, "dve")

    emit_ln_fm(p, pm, lambda c: HSv[c], T, sel_a, sel_b, ones_row,
               out_fn=lambda c: H[c][:], gamma=lambda c: lng_sb[:, c:c + 1],
               beta=lambda c: lnb_sb[:, c:c + 1], sel_x=co.ones_bfc[:])

    G1 = [HS[:, 512 * j:512 * (j + 1)] for j in range(22)]
    wst = {}
    for ts in range(4):
        tsl = slice(512 * ts, 512 * (ts + 1))
        for i_ in range(22):
            bufa = W1b[0]
            bufg = W1b[1]
            if id(bufa) in wst:
                p.wait(nc.gpsimd, "pe", wst[id(bufa)])
            p.loadc(bufa[:], w1t[i_, :, :], sem="wdma")
            if id(bufg) in wst:
                p.wait(nc.gpsimd, "pe", wst[id(bufg)])
            p.loadc(bufg[:], w1t[22 + i_, :, :], sem="wdma")
            p.wait(nc.tensor, "wdma")
            p.wait(nc.tensor, "dve")
            bank_a, pva = pm.acquire(nc.tensor)
            for kc in range(NCH):
                i = nc.tensor.matmul(pva, bufa[:, 128 * kc:128 * (kc + 1)],
                                     H[kc][:, tsl], start=(kc == 0), stop=(kc == 7))
            p.inc(i, "pe")
            bank_g, pvg = pm.acquire(nc.tensor)
            for kc in range(NCH):
                i = nc.tensor.matmul(pvg, bufg[:, 128 * kc:128 * (kc + 1)],
                                     H[kc][:, tsl], start=(kc == 0), stop=(kc == 7))
            wst[id(bufa)] = wst[id(bufg)] = p.inc(i, "pe")
            p.wait(nc.scalar, "pe")
            p.inc(nc.scalar.activation(out=gg[:], in_=pvg, func=AF.Gelu,
                                       bias=b1_sb[:, 22 + i_:23 + i_], scale=1.0), "act")
            p.wait(nc.vector, "act")
            p.inc(nc.vector.scalar_tensor_tensor(G1[i_], pva, b1_sb[:, i_:i_ + 1],
                                                 gg[:], ALU.add, ALU.mult), "dve")
            pm.release(bank_a, "dve")
            pm.release(bank_g, "act")
        if id(W2b) in wst:
            p.wait(nc.gpsimd, "pe", wst[id(W2b)])
        p.loadc(W2b[:], w2t[0, :, :], sem="wdma")
        for oc in range(NCH):
            if oc > 0:
                p.wait(nc.gpsimd, "pe", wst.get(id(W2b), 0))
                p.loadc(W2b[:], w2t[oc, :, :], sem="wdma")
            p.wait(nc.tensor, "wdma")
            p.wait(nc.tensor, "dve")
            bank, pv = pm.acquire(nc.tensor)
            for kc in range(22):
                i = nc.tensor.matmul(pv, W2b[:, 128 * kc:128 * (kc + 1)], G1[kc],
                                     start=(kc == 0), stop=(kc == 21))
            wst[id(W2b)] = p.inc(i, "pe")
            p.wait(nc.vector, "pe")
            p.inc(nc.vector.scalar_tensor_tensor(
                Z[oc][:, tsl], pv, b2_sb[:, oc:oc + 1], Z[oc][:, tsl],
                ALU.add, ALU.add), "dve")
            pm.release(bank, "dve")
        if not last:
            p.wait(nc.sync, "dve")
            for c in range(NCH):
                p.store(ys[128 * c:128 * (c + 1), tsl], Z[c][:, tsl])
    if last:
        for ts in range(4):
            tsl = slice(512 * ts, 512 * (ts + 1))
            for c in range(NCH):
                p.load(Xs[c][:], xs[128 * c:128 * (c + 1), tsl])
            for e in (nc.scalar, nc.vector, nc.tensor):
                p.wait(e, "dma")
            emit_ln_fm(p, pm, lambda c, _t=tsl: Z[c][:, _t], 512, sel_a, sel_b,
                       ones_row, res_fn=lambda c: Xs[c][:])
            p.wait(nc.sync, "dve")
            for c in range(NCH):
                p.store(ys[128 * c:128 * (c + 1), tsl], Xs[c][:])
    p.barrier()
    seg_cm.__exit__(None, None, None)


def build_C_ffn(fam, last=False):
    p = Prog()
    zs = p.din("zs", [D, T])
    zh1 = p.din("zh1", [128, 8])
    xs = p.din("xs", [D, T]) if last else None
    ys = p.dout("ys", [D, T])
    p.ln_tmx = 2048
    p.start()
    csb, sel_a, sel_b, eye, ones_row, co = load_consts(p)
    pm = PsumPool(p, banks=8)
    seg_C_ffn(p, pm, co, lambda n, s, d=F32: p.din(n, s, d), zs,
              None if last else ys, zh1, fam, last=last, xin=xs,
              xout=ys if last else None)
    return p


FAM_L1 = {"W": 512, "HEADS": 8, "INNER": 512}
FAM_L2 = {"W": 256, "HEADS": 16, "INNER": 1024}


def prep_inner_attn(fam, wqkv, wo, q_scale, k_scale):
    W, INNER = fam["W"], fam["INNER"]
    wqkv = np.asarray(wqkv, np.float32)
    wq = wqkv[:, 0:INNER]
    wk = wqkv[:, INNER:2 * INNER]
    wvn = np.ascontiguousarray(wqkv[:, 2 * INNER:3 * INNER])
    wqkvt = np.concatenate([tile_w(wq), tile_w(wk)], axis=0)
    wot = tile_w(np.asarray(wo, np.float32))
    def arr2(v):
        v = np.asarray(v, np.float32).reshape(64)
        a = np.concatenate([v, v])
        b = np.concatenate([v[32:64], v[0:32], v[32:64], v[0:32]])
        return np.ascontiguousarray(np.stack([a, b], axis=1))
    out = {"wqkvt": wqkvt, "wvn": wvn, "wot": wot,
           "qsr": arr2(q_scale), "ksr": arr2(k_scale),
           "gates": host_gates(W, 0), "selpb": host_selpb(),
           "consts": host_consts()}
    percore = []
    for c in range(NCORE):
        t0 = (c % 2) * T
        cq, sq_ = rot_tables(W, t0, t0 + T, "q")
        ck, sk_ = rot_tables(W, t0 - W, t0 + T, "k")
        if c % 2 == 0:
            ck[:, 0:W] = 0.0
            sk_[:, 0:W] = 0.0
        percore.append({
            "rotq": np.ascontiguousarray(np.concatenate([cq, sq_], axis=1)),
            "rotk": np.ascontiguousarray(np.concatenate([ck, sk_], axis=1)),
            "padm": np.full((128, 1), 0.0 if c % 2 else -30000.0, np.float32)})
    return out, percore


def run_inner_layer(pA, pF, fam, z_t, l, wq, wo, qs, ks, lng, lnb, w1, b1, w2, b2,
                    x_t=None, last=False):
    W = fam["W"]
    HB = W + 1
    shared, percore = prep_inner_attn(fam, wq, wo, qs, ks)
    in_maps = []
    for c in range(NCORE):
        b, sl = core_slice(c)
        if c % 2 == 0:
            zh = np.zeros((D, HB), np.float32)
        else:
            zh = np.ascontiguousarray(z_t[b, :, T - HB:T])
        in_maps.append(dict(shared, zs=np.ascontiguousarray(z_t[b, :, sl]),
                            zh=zh, **percore[c]))
    resA = pA.run(in_maps)
    z2 = gather_stream(resA.results)
    HID = 2730
    w1 = np.asarray(w1, np.float32)
    b1 = np.asarray(b1, np.float32)
    w1p = np.zeros((D, 5632), np.float32)
    w1p[:, 0:HID] = w1[:, 0:HID]
    w1p[:, 2816:2816 + HID] = w1[:, HID:2 * HID]
    b1p = np.zeros(5632, np.float32)
    b1p[0:HID] = b1[0:HID]
    b1p[2816:2816 + HID] = b1[HID:2 * HID]
    w2p = np.zeros((2816, D), np.float32)
    w2p[0:HID, :] = np.asarray(w2, np.float32)
    ffn_in = {"lngr": col128(lng), "lnbr": col128(lnb),
              "w1t": tile_w(w1p), "b1r": col128(b1p),
              "w2t": tile_w(w2p), "b2r": col128(b2),
              "consts": host_consts()}
    in_maps = []
    for c in range(NCORE):
        b, sl = core_slice(c)
        if c % 2 == 0:
            zh1 = np.zeros((128, 8), np.float32)
        else:
            zh1 = col128(z2[b, :, T - 1])
        m = dict(ffn_in, zs=np.ascontiguousarray(z2[b, :, sl]), zh1=zh1)
        if last:
            m["xs"] = np.ascontiguousarray(x_t[b, :, sl])
        in_maps.append(m)
    resF = pF.run(in_maps)
    return gather_stream(resF.results)


def build_Z0():
    p = Prog()
    xs = p.din("xs", [D, T])
    us = p.dout("us", [D, T])
    p.start()
    csb, sel_a, sel_b, eye, ones_row, co = load_consts(p)
    co.sel_a, co.sel_b, co.ones_row = sel_a, sel_b, ones_row
    pm = PsumPool(p, banks=8)
    seg_U(p, pm, co, xs, us, F32)
    return p


def run_inner_block(pZ, pA, pF, pFlast, fam, x_t, weights, depth):
    shards = shard_stream(x_t)
    resZ = pZ.run([{"xs": shards[c], "consts": host_consts()} for c in range(NCORE)])
    z_t = gather_stream(resZ.results, "us")
    for l in range(depth):
        wq, qs, ks, wo, lng, lnb, w1, b1, w2, b2 = [w[l] for w in weights]
        last = (l == depth - 1)
        z_t = run_inner_layer(pA, pFlast if last else pF, fam, z_t, l,
                              wq, wo, qs, ks, lng, lnb, w1, b1, w2, b2,
                              x_t=x_t, last=last)
    return z_t


_PROGS = {}


def _prog(name, fn):
    if name not in _PROGS:
        _PROGS[name] = fn()
    return _PROGS[name]


def kernel(**inputs):
    return kernel_mega(**inputs)


def kernel_multilaunch(**inputs):
    inp = {k: np.asarray(v) for k, v in inputs.items()}
    x_t = np.ascontiguousarray(inp["L0_attn_x"].astype(np.float32).transpose(0, 2, 1))

    pU = _prog("U", build_U)
    x_t = run_L0_attn(pU, _prog("L0a", build_L0_attn), x_t, inp)
    p1 = _prog("L0f", build_L0_ffn)
    x_t = run_L0_ffn(p1, x_t, inp)
    x_t = run_L1_attn(pU, _prog("L1s", build_L1a_scan), _prog("L1o", build_L1a_out),
                      x_t, inp)
    w1 = [inp["L1_ffn_" + k].astype(np.float32) for k in
          ["wqkv", "q_scale", "k_scale", "wo", "ln_g", "ln_b", "w1", "b1", "w2", "b2"]]
    x_t = run_inner_block(_prog("Z0", build_Z0),
                          _prog("C1a", lambda: build_C_attn(FAM_L1)),
                          _prog("C1f", lambda: build_C_ffn(FAM_L1, last=False)),
                          _prog("C1fl", lambda: build_C_ffn(FAM_L1, last=True)),
                          FAM_L1, x_t, w1, 2)
    x_t = run_L2_attn(pU, _prog("L2q", build_L2a_qkv), _prog("L2a", build_L2a_attn),
                      x_t, inp)
    w2 = [inp["L2_ffn_" + k].astype(np.float32) for k in
          ["Wqkv", "q_scale", "k_scale", "Wo", "ln_g", "ln_b", "W1", "b1", "W2", "b2"]]
    x_t = run_inner_block(_prog("Z0", build_Z0),
                          _prog("C2a", lambda: build_C_attn(FAM_L2)),
                          _prog("C2f", lambda: build_C_ffn(FAM_L2, last=False)),
                          _prog("C2fl", lambda: build_C_ffn(FAM_L2, last=True)),
                          FAM_L2, x_t, w2, 6)
    x_t = run_L3_attn(pU, _prog("L3l", build_L3a_local), _prog("L3o", build_L3a_out),
                      x_t, inp)
    x_t = run_L3_ffn(pU, _prog("L3f", build_L3_ffn), x_t, inp)

    return np.ascontiguousarray(x_t.transpose(0, 2, 1))


def build_mega(debug=False):
    p = Prog()
    xs = p.din("xs", [D, T])
    ys = p.dout("ys", [D, T])
    hfc_in = p.din("hfc", [128, 1])
    snaps = p.dout("snaps", [8, D, T]) if debug else None
    snap_i = [0]

    def snap():
        if debug:
            p.store(snaps[snap_i[0], :, :], xcur[:, :])
            p.barrier()
        snap_i[0] += 1
    p.ln_tmx = 2048
    p.start()
    nc = p.nc
    csb, sel_a, sel_b, eye, ones_row, co = load_consts(p)
    for nm in ("dma", "wdma", "sto", "dve", "act", "pe", "pool", "rot", "zre",
               "yl", "ccs", "cc", "bar_t", "bar_v", "bar_s", "bar_g", "bar_y"):
        p.sem(nm)
    pm = PsumPool(p, banks=5)
    P1 = p.psum("P1", [128, 512], F32)
    P2 = p.psum("P2", [128, 512], F32)
    P3 = p.psum("P3", [128, 512], F32)
    hfc_sb = p.sb("hfc_sb", [128, 1], F32)
    p.load(hfc_sb[:], hfc_in[:, :])
    hfc = hfc_sb[:, 0:1]

    xcur = p.dint("xcur", [D, T])
    zcur = p.dint("zcur", [D, T])
    u_loc = p.dint("u_loc", [D, T], BF16)

    def pdin(pfx):
        return lambda n, s, d=F32, _pf=pfx: p.din(_pf + n, s, d)

    p.store(xcur, xs[:, :])
    p.barrier()

    seg_U(p, pm, co, xcur, u_loc, BF16)
    uall = p.ag_dram("agu0", u_loc, [D, T], BF16)
    p.barrier()
    seg_L0_attn(p, pm, co, pdin("l0a_"), P1, P2, xcur, xcur, u_loc, uall)
    snap()
    seg_L0_ffn(p, pm, co, pdin("l0f_"), xcur, xcur)
    snap()
    seg_U(p, pm, co, xcur, u_loc, BF16)
    hsl = p.dint("hsl", [2, 128, T])
    st1 = p.dint("st1", [128, 2])
    seg_L1a_scan(p, pm, pdin("l1a_"), u_loc, hsl, st1)
    st1all = p.ag_dram("agst1", st1, [128, 2], F32)
    p.barrier()
    seg_L1a_out(p, pm, co, pdin("l1a_"), xcur, xcur, u_loc, hsl,
                st1all[0], hfc=hfc)
    snap()

    def inner(famname, fam, depth):
        HB = fam["W"] + 1
        seg_U(p, pm, co, xcur, zcur, F32)
        for l in range(depth):
            pfx = "%s_%d_" % (famname, l)
            ztail = p.dint(p.uid("zt"), [D, HB])
            p.store(ztail, zcur[:, T - HB:T], sem="ccs")
            p.wait(nc.gpsimd, "ccs")
            zhall = p.ag_dram(p.uid("agzh"), ztail, [D, HB], F32)
            p.barrier()
            seg_C_attn(p, pm, co, pdin(pfx), P1, P2, zcur, zcur, zhall[0],
                       fam, hfc=hfc)
            zt1 = p.dint(p.uid("z1"), [D, 1])
            p.store(zt1, zcur[:, T - 1:T], sem="ccs",
                    allow_slow_non_contiguous=True)
            p.wait(nc.gpsimd, "ccs")
            z1all = p.ag_dram(p.uid("agz1"), zt1, [D, 1], F32)
            p.barrier()
            last = (l == depth - 1)
            seg_C_ffn(p, pm, co, pdin(pfx), zcur, zcur,
                      z1all[0].rearrange("(c q) o -> q (c o)", q=128), fam,
                      last=last, xin=xcur if last else None,
                      xout=xcur if last else None, hfc=hfc)

    inner("c1", FAM_L1, 2)
    snap()

    seg_U(p, pm, co, xcur, u_loc, BF16)
    qs_l = p.dint("qs_l", [D, T], BF16)
    ks_l = p.dint("ks_l", [D, T], BF16)
    vs_l = p.dint("vs_l", [16, 128, 1040], BF16)
    seg_L2a_qkv(p, pm, pdin("l2a_"), u_loc, qs_l, ks_l, vs_l)
    kall = p.ag_dram("agk", ks_l, [D, T], BF16)
    vall = p.ag_dram("agv", vs_l, [16, 128, 1040], BF16)
    p.barrier()
    seg_L2a_attn(p, pm, co, pdin("l2a_"), P1, P2, xcur, xcur, qs_l,
                 kall[0], ks_l, vall[0], vs_l)
    snap()

    inner("c2", FAM_L2, 6)
    snap()

    seg_U(p, pm, co, xcur, u_loc, BF16)
    oaug = p.dint("oaug", [16, 65, T], BF16)
    qp_l = p.dint("qp_l", [D, T], BF16)
    st3 = p.dint("st3", [128, 8 * 130], BF16)
    seg_L3a_local(p, pm, pdin("l3a_"), P1, P2, P3, u_loc, oaug, qp_l, st3)
    st3all = p.ag_dram("agst3", st3, [128, 8 * 130], BF16)
    p.barrier()
    seg_L3a_out(p, pm, co, pdin("l3a_"), P2, xcur, xcur, qp_l, oaug,
                st3all[0], hfc=hfc)
    snap()

    seg_U(p, pm, co, xcur, u_loc, BF16)

    def s_ag(s_or_sb):
        sint = p.dint("s_own", [1, T])
        p.wait(nc.sync, "dve")
        p.store(sint, s_or_sb, sem="ccs")
        p.wait(nc.gpsimd, "ccs")
        return p.ag_dram("ags", sint, [1, T], F32)

    seg_L3_ffn(p, pm, co, pdin("l3f_"), P2, P3, xcur, xcur, u_loc, s_ag=s_ag,
               debug=debug)
    snap()

    p.store(ys[:, :], xcur)
    return p


def prep_mega(inp):
    cst = host_consts()
    shared = {"consts": cst}
    wbd = np.zeros((128, 3 * 128), np.float32)
    for j, nm in enumerate(["L0_attn_Wq", "L0_attn_Wk", "L0_attn_Wv"]):
        Wm = np.asarray(inp[nm], np.float32)
        wbd[0:64, 128 * j:128 * j + 64] = Wm
        wbd[64:128, 128 * j + 64:128 * j + 128] = Wm
    shared.update({"l0a_wbd": wbd, "l0a_wot": tile_w(inp["L0_attn_Wo"]),
                   "l0a_bor": col128(inp["L0_attn_bo"])})
    shared.update({"l0f_w1t": tile_w(inp["L0_ffn_W1"]),
                   "l0f_b1r": col128(inp["L0_ffn_b1"]),
                   "l0f_w2t": tile_w(inp["L0_ffn_W2"]),
                   "l0f_b2r": col128(inp["L0_ffn_b2"])})
    shared.update({
        "l1a_bwt": tile_w(np.ascontiguousarray(np.asarray(inp["L1_attn_B_w"]).T)),
        "l1a_cwt": tile_w(np.ascontiguousarray(np.asarray(inp["L1_attn_C_w"]).T)),
        "l1a_lamr": col128(inp["L1_attn_log_lambda"]),
        "l1a_dvr": col128(inp["L1_attn_D"])})
    percore_inner = [dict() for _ in range(NCORE)]
    for famname, fam, depth, wkeys in (
            ("c1", FAM_L1, 2, ["L1_ffn_wqkv", "L1_ffn_q_scale", "L1_ffn_k_scale",
                               "L1_ffn_wo", "L1_ffn_ln_g", "L1_ffn_ln_b",
                               "L1_ffn_w1", "L1_ffn_b1", "L1_ffn_w2", "L1_ffn_b2"]),
            ("c2", FAM_L2, 6, ["L2_ffn_Wqkv", "L2_ffn_q_scale", "L2_ffn_k_scale",
                               "L2_ffn_Wo", "L2_ffn_ln_g", "L2_ffn_ln_b",
                               "L2_ffn_W1", "L2_ffn_b1", "L2_ffn_W2", "L2_ffn_b2"])):
        wq, qsc, ksc, wo, lng, lnb, w1, b1, w2, b2 = [
            np.asarray(inp[k], np.float32) for k in wkeys]
        HID = 2730
        for l in range(depth):
            pfx = "%s_%d_" % (famname, l)
            sh, percore = prep_inner_attn(fam, wq[l], wo[l], qsc[l], ksc[l])
            for k in ("wqkvt", "wvn", "wot", "qsr", "ksr", "gates", "selpb"):
                shared[pfx + k] = sh[k]
            for c in range(NCORE):
                for k in ("rotq", "rotk", "padm"):
                    percore_inner[c][pfx + k] = percore[c][k]
            w1p = np.zeros((D, 5632), np.float32)
            w1p[:, 0:HID] = w1[l][:, 0:HID]
            w1p[:, 2816:2816 + HID] = w1[l][:, HID:2 * HID]
            b1p = np.zeros(5632, np.float32)
            b1p[0:HID] = b1[l][0:HID]
            b1p[2816:2816 + HID] = b1[l][HID:2 * HID]
            w2p = np.zeros((2816, D), np.float32)
            w2p[0:HID, :] = w2[l]
            shared.update({pfx + "lngr": col128(lng[l]), pfx + "lnbr": col128(lnb[l]),
                           pfx + "w1t": tile_w(w1p), pfx + "b1r": col128(b1p),
                           pfx + "w2t": tile_w(w2p), pfx + "b2r": col128(b2[l])})
    Wq = np.asarray(inp["L2_attn_Wq"], np.float32).transpose(1, 0, 2).reshape(D, D)
    Wk = np.asarray(inp["L2_attn_Wk"], np.float32).transpose(1, 0, 2).reshape(D, D)
    Wv = np.asarray(inp["L2_attn_Wv"], np.float32).transpose(1, 0, 2).reshape(D, D)
    shared.update({
        "l2a_wqt": tile_w(Wq), "l2a_wkt": tile_w(Wk),
        "l2a_wvn": np.ascontiguousarray(Wv),
        "l2a_bqr": col128(np.asarray(inp["L2_attn_bq"]).reshape(-1)),
        "l2a_bkr": col128(np.asarray(inp["L2_attn_bk"]).reshape(-1)),
        "l2a_bvrow": np.ascontiguousarray(
            np.asarray(inp["L2_attn_bv"], np.float32).reshape(1, D)),
        "l2a_m4": host_diag_masks(),
        "l2a_wot": tile_w(inp["L2_attn_Wo"]), "l2a_bor": col128(inp["L2_attn_bo"])})
    proj = np.asarray(inp["L3_attn_proj"], np.float32)
    projp = np.zeros((128, 8 * 64), np.float32)
    for c in range(NCH):
        projp[0:64, 64 * c:64 * (c + 1)] = proj[2 * c]
        projp[64:128, 64 * c:64 * (c + 1)] = proj[2 * c + 1]
    wqkv3 = np.asarray(inp["L3_attn_w_qkv"], np.float32)
    shared.update({
        "l3a_wqkvt": tile_w(wqkv3),
        "l3a_bqkr": col128(np.asarray(inp["L3_attn_b_qkv"], np.float32)[0:2048]),
        "l3a_bvrow": np.ascontiguousarray(
            np.asarray(inp["L3_attn_b_qkv"], np.float32)[2048:3072].reshape(1, D)),
        "l3a_wvn": np.ascontiguousarray(wqkv3[:, 2048:3072]),
        "l3a_projp": projp,
        "l3a_mup": np.triu(np.ones((128, 128), np.float32)),
        "l3a_wot": tile_w(inp["L3_attn_w_out"]),
        "l3a_bor": col128(inp["L3_attn_b_out"])})
    shared.update({
        "l3f_tokr": col128(inp["L3_ffn_routing_token"]),
        "l3f_glr": col128(inp["L3_ffn_gamma_l"]),
        "l3f_ghr": col128(inp["L3_ffn_gamma_h"]),
        "l3f_w1lt": tile_w(np.ascontiguousarray(
            np.asarray(inp["L3_ffn_w1_l"], np.float32).T)),
        "l3f_b1lr": col128(inp["L3_ffn_b1_l"]),
        "l3f_w2lt": tile_w(np.ascontiguousarray(
            np.asarray(inp["L3_ffn_w2_l"], np.float32).T)),
        "l3f_b2lr": col128(inp["L3_ffn_b2_l"]),
        "l3f_w1ht": tile_w(np.ascontiguousarray(
            np.asarray(inp["L3_ffn_w1_h"], np.float32).T)),
        "l3f_b1hr": col128(inp["L3_ffn_b1_h"]),
        "l3f_w2ht": tile_w(np.ascontiguousarray(
            np.asarray(inp["L3_ffn_w2_h"], np.float32).T)),
        "l3f_b2hr": col128(inp["L3_ffn_b2_h"])})
    x_t = np.ascontiguousarray(
        np.asarray(inp["L0_attn_x"], np.float32).transpose(0, 2, 1))
    shards = shard_stream(x_t)
    in_maps = []
    for c in range(NCORE):
        half = c % 2
        m = dict(shared)
        m.update(percore_inner[c])
        m["xs"] = shards[c]
        m["hfc"] = np.full((128, 1), float(half), np.float32)
        m["l2a_padm"] = (np.concatenate(
            [np.full((128, 16), -30000.0, np.float32),
             np.zeros((128, 16), np.float32)], axis=1) if half == 0
            else np.zeros((128, 32), np.float32))
        m["l3f_hf"] = np.full((1, 1), float(half), np.float32)
        in_maps.append(m)
    return in_maps


def kernel_mega(**inputs):
    inp = {k: np.asarray(v) for k, v in inputs.items()}
    pm_ = _prog("MEGA", build_mega)
    in_maps = prep_mega(inp)
    res = pm_.run(in_maps)
    x = gather_stream(res.results)
    return np.ascontiguousarray(x.transpose(0, 2, 1))
